# version 01
# baseline (speedup 1.0000x reference)
import numpy as np
from contextlib import ExitStack

import concourse.bass as bass
import concourse.mybir as mybir
from concourse.bass_utils import run_bass_kernel_spmd
from concourse.aot_env import allow_ici_all_to_all

F32 = mybir.dt.float32
BF16 = mybir.dt.bfloat16
F16 = mybir.dt.float16
I32 = mybir.dt.int32
AF = mybir.ActivationFunctionType
ALU = mybir.AluOpType
AX = mybir.AxisListType

NC = 8
T = 8192
TL = T // NC
E = 2048
DH = 16
HD = 64
DV = 128
DI = 4096
DXB = 1024
DST = 16
DTR = 128
INTER = 8192
IQ = 5632
PDIM = 2 * DI + 2 * DXB + DTR
DSL = DI // NC
LAM_INIT = float(0.8 - 0.6 * np.exp(-0.3 * 12))
SHIFT = 4.0

_UID = [0]
def _uname(pfx):
    _UID[0] += 1
    return f"{pfx}{_UID[0]}"

STOP_AFTER = None
DEBUG_TAPS = ()


class Prog:
    def __init__(self, nc, stack):
        self.nc = nc
        self.stack = stack
        self.sem = {}
        self.cnt = {}

    def new_sem(self, name):
        h = self.stack.enter_context(self.nc.semaphore(name))
        self.sem[name] = h
        self.cnt[name] = 0
        return h

    def inc(self, inst, name, n=1):
        inst.then_inc(self.sem[name], n)
        self.cnt[name] += n
        return self.cnt[name]

    def wait(self, eng, name, val=None):
        eng.wait_ge(self.sem[name], self.cnt[name] if val is None else val)


def build_program():
    nc = bass.Bass("TRN2", target_bir_lowering=False, debug=False)

    def inp(name, shape, dtype=F32):
        return nc.dram_tensor(name, list(shape), dtype, kind="ExternalInput").ap()

    x0_in = inp("x0", [TL, E])
    wq_in = inp("wq_s", [256, E])
    wk_in = inp("wk_s", [256, E])
    wv_in = inp("wv_s", [256, E])
    wo_in = inp("wo_s", [256, E])
    w_in_in = inp("w_in_s", [PDIM, 256])
    gate_in = inp("gate_s", [INTER, 256])
    up_in = inp("up_s", [INTER, 256])
    down_in = inp("down_s", [E, 1024])
    outp_in = inp("outp_s", [E, 512])
    gup_in = inp("gup_s", [2 * IQ, 32], I32)
    gus_in = inp("gus_s", [2 * IQ, 2], F16)
    dnp_in = inp("dnp_s", [256, IQ // 8], I32)
    dns_in = inp("dns_s", [256, 44], F16)
    dtw_in = inp("dtw_s", [DSL, DTR])
    convw_in = inp("convw_s", [128, 16])
    convb_in = inp("convb_s", [128, 4])
    alog_in = inp("alog_s", [128, 64])
    dpar_in = inp("dpar_s", [128, 4])
    dtb_in = inp("dtb_s", [128, 4])
    ln1_in = inp("ln1", [1, E])
    ln2_in = inp("ln2", [1, E])
    lq1_in = inp("lq1", [1, HD]); lk1_in = inp("lk1", [1, HD])
    lq2_in = inp("lq2", [1, HD]); lk2_in = inp("lk2", [1, HD])
    iden_in = inp("iden", [128, 128])

    out_t = nc.dram_tensor("out", [TL, E], F32, kind="ExternalOutput").ap()
    tapd = {}
    for tname in DEBUG_TAPS:
        shp = {"ln0T": [128, 16 * TL], "qkvT": [NC * 768, TL],
               "xin": [TL, E], "attO": [NC * 256, TL],
               "fh2": [128, 16 * 512], "fhm": [128, 11 * 512],
               "fya": [128, 8192], "wdnchk": [IQ, 256], "wguchk": [256, 2 * IQ],
               "wdg": [128, 22528],
               "x1": [TL, E], "x2": [TL, E], "zx": [NC * 768, TL],
               "ys": [NC * DSL, TL], "r1": [TL, E], "x3": [TL, E]}[tname]
        dt_ = BF16 if tname in ("ln0T", "qkvT", "zx", "ys", "attO", "fh2", "fhm", "wdnchk", "wguchk", "wdg") else F32
        tapd[tname] = nc.dram_tensor("tap_" + tname, shp, dt_,
                                     kind="ExternalOutput").ap()

    def idram(name, shape, dtype=BF16):
        return nc.dram_tensor(name, list(shape), dtype)

    agw = {}
    def ag_pair(name, rows_per_core, cols):
        agw[name] = (idram("agi_" + name, [rows_per_core, cols]),
                     idram("ago_" + name, [NC * rows_per_core, cols]))
    ag_pair("wq", 256, E); ag_pair("wk", 256, E)
    ag_pair("wv", 256, E); ag_pair("wo", 256, E)
    ag_pair("w_in", 256, PDIM)
    ag_pair("gate", 256, INTER); ag_pair("up", 256, INTER)
    ag_pair("down", 1024, E)
    ag_pair("outp", 512, E)
    ag_pair("wgu", 256, 2 * IQ)
    ag_pair("wdn", IQ, 256)

    a2a_qkv_i = idram("a2a_qkv_i", [NC * 768, TL])
    a2a_qkv_o = idram("a2a_qkv_o", [NC * 768, TL])
    a2a_o_i = idram("a2a_o_i", [NC * 256, TL])
    a2a_o_o = idram("a2a_o_o", [NC * 256, TL])
    a2a_sc_i = idram("a2a_sc_i", [NC * 768, TL])
    a2a_sc_o = idram("a2a_sc_o", [NC * 768, TL])
    ag_dt_i = idram("ag_dt_i", [DTR, TL])
    ag_dt_o = idram("ag_dt_o", [NC * DTR, TL])
    a2a_y_i = idram("a2a_y_i", [NC * DSL, TL])
    a2a_y_o = idram("a2a_y_o", [NC * DSL, TL])
    z_spill = idram("z_spill", [DI, TL])
    x2_spill = idram("x2_spill", [TL, E], F32)

    stack = ExitStack()
    with stack:
        p = Prog(nc, stack)
        for s in ("g", "s", "t", "v", "a", "q"):
            p.new_sem(s)
        ccsems = [p.new_sem(f"cc{i}") for i in range(20)]
        cc_n = [0]

        gp, sy, te, ve, ac = nc.gpsimd, nc.sync, nc.tensor, nc.vector, nc.scalar

        def collective(kind, ins, outs):
            sem = ccsems[cc_n[0]]; cc_n[0] += 1
            if kind == "AllToAll":
                with allow_ici_all_to_all():
                    inst = gp.collective_compute(
                        kind, ALU.bypass, replica_groups=[list(range(NC))],
                        ins=ins, outs=outs)
            else:
                inst = gp.collective_compute(
                    kind, ALU.bypass, replica_groups=[list(range(NC))],
                    ins=ins, outs=outs)
            inst.then_inc(sem)
            return sem

        sbg = lambda *a, **k: stack.enter_context(nc.sbuf_tensor(*a, **k))

        def V(inst):
            p.inc(inst, "v"); p.wait(ve, "v")
        def A(inst):
            p.inc(inst, "a"); p.wait(ac, "a")
        def Q(inst):
            p.inc(inst, "q"); p.wait(gp, "q")
        def Tm(inst):
            p.inc(inst, "t")
        def dma(eng, out, in_, name="g"):
            p.inc(eng.dma_start(out=out, in_=in_), name, 16)

        R = dict(side="right")
        iden_bf = sbg("iden_bf", [128, 128], BF16, **R)
        ones_col = sbg("ones_col", [128, 1], BF16, **R)
        ones_row = sbg("ones_row", [1, 128], BF16, **R)
        lam_sb = sbg("lam_sb", [1, 8], F32, **R)
        small = sbg("small", [1, 512], F32, **R)
        col = sbg("colstats", [128, 16], F32, **R)
        sq_scr = sbg("sq_scr", [128, E], BF16, **R)
        h_nat = sbg("h_nat", [128, E], BF16, **R)
        stg = sbg("stg", [128, 1024], BF16, **R)
        dtwT = sbg("dtwT", [128, 512], BF16, **R)
        attc = sbg("attc", [128, 2], F32, **R)

        shared = {"tr_ps": None, "x": None}

        dma(gp, iden_bf[:, :], iden_in)
        V(ve.memset(ones_col[:, :], 1.0))
        V(ve.memset(ones_row[:, :], 1.0))
        V(ve.memset(attc[:, 0:1], -SHIFT))
        V(ve.memset(attc[:, 1:2], 0.125))
        ac.wait_ge(p.sem["v"], p.cnt["v"])
        dma(gp, small[0:1, 0:HD], lq1_in)
        dma(gp, small[0:1, HD:2 * HD], lk1_in)
        dma(gp, small[0:1, 2 * HD:3 * HD], lq2_in)
        dma(gp, small[0:1, 3 * HD:4 * HD], lk2_in)
        p.wait(ve, "g")
        V(ve.tensor_mul(small[0:1, 256:256 + HD], small[0:1, 0:HD],
                        small[0:1, HD:2 * HD]))
        V(ve.tensor_mul(small[0:1, 320:320 + HD], small[0:1, 2 * HD:3 * HD],
                        small[0:1, 3 * HD:4 * HD]))
        V(ve.reduce_sum(small[0:1, 384:385], small[0:1, 256:256 + HD], axis=AX.X))
        V(ve.reduce_sum(small[0:1, 385:386], small[0:1, 320:320 + HD], axis=AX.X))
        p.wait(ac, "v")
        A(ac.activation(small[0:1, 388:390], small[0:1, 384:386], AF.Exp))
        p.wait(ve, "a")
        V(ve.tensor_sub(small[0:1, 390:391], small[0:1, 388:389],
                        small[0:1, 389:390]))
        V(ve.tensor_scalar_add(lam_sb[0:1, 0:1], small[0:1, 390:391], LAM_INIT))

        def emit_rstd(src_view, mean_sub, eps):
            p.wait(ac, "v")
            A(ac.activation(sq_scr[:, :], src_view, AF.Square,
                            accum_out=col[:, 0:1]))
            if mean_sub:
                V(ve.reduce_sum(col[:, 1:2], src_view, axis=AX.X))
                p.wait(ve, "a")
                V(ve.tensor_scalar_mul(col[:, 2:3], col[:, 1:2], 1.0 / E))
                V(ve.tensor_scalar_mul(col[:, 3:4], col[:, 0:1], 1.0 / E))
                V(ve.tensor_mul(col[:, 4:5], col[:, 2:3], col[:, 2:3]))
                V(ve.tensor_tensor(col[:, 5:6], col[:, 3:4], col[:, 4:5],
                                   op=ALU.subtract))
            else:
                p.wait(ve, "a")
                V(ve.tensor_scalar_mul(col[:, 5:6], col[:, 0:1], 1.0 / E))
            V(ve.tensor_scalar_add(col[:, 6:7], col[:, 5:6], eps))
            p.wait(ac, "v")
            A(ac.sqrt(col[:, 7:8], col[:, 6:7]))
            p.wait(ve, "a")
            V(ve.reciprocal(col[:, 8:9], col[:, 7:8]))

        def emit_ln(src_view, dst_bf, mean_sub=True, eps=1e-5):
            emit_rstd(src_view, mean_sub, eps)
            if mean_sub:
                V(ve.tensor_scalar(dst_bf, src_view, col[:, 2:3], col[:, 8:9],
                                   op0=ALU.subtract, op1=ALU.mult))
            else:
                V(ve.tensor_scalar_mul(dst_bf, src_view, col[:, 8:9]))

        def nat_to_T(src_bf, dst_col_fn):
            for half in range(2):
                pb = shared["tr_ps"][half]
                p.wait(te, "v")
                p.wait(te, "a")
                for c8 in range(8):
                    c = half * 8 + c8
                    inst = te.transpose(pb[:, c8 * 128:(c8 + 1) * 128],
                                        src_bf[:, c * 128:(c + 1) * 128],
                                        iden_bf[:, :])
                    p.inc(inst, "t")
                p.wait(ac, "t")
                for c8 in range(8):
                    c = half * 8 + c8
                    inst = ac.copy(dst_col_fn(c), pb[:, c8 * 128:(c8 + 1) * 128])
                    p.inc(inst, "a")
                p.wait(ac, "a")

        def emit_ln_bc(row_in, dst_bc, bc_ps):
            dma(gp, sq_scr[0:1, 0:E], row_in)
            p.wait(te, "g")
            for cq in range(4):
                p.wait(te, "a")
                inst = te.matmul(bc_ps[:, :], ones_row[0:1, :],
                                 sq_scr[0:1, cq * 512:(cq + 1) * 512],
                                 start=True, stop=True)
                p.inc(inst, "t")
                p.wait(ac, "t")
                A(ac.copy(dst_bc[:, cq * 512:(cq + 1) * 512], bc_ps[:, :]))

        def fence():
            for eng in (gp, sy, te, ve, ac):
                for s in ("t", "v", "a", "q", "g", "s"):
                    eng.wait_ge(p.sem[s], p.cnt[s])

        def final_output_and_done():
            fence()
            x_sb = shared["x"]
            for j in range(8):
                dma(sy, out_t[j * 128:(j + 1) * 128, :],
                    x_sb[:, j * E:(j + 1) * E], "s")
            p.wait(sy, "s")
            p.wait(gp, "s")

        ag_sems = {}
        with ExitStack() as ph:
            en = ph.enter_context
            pt_src = en(nc.sbuf_tensor("pt_src", [128, 1024], BF16))
            pt_stage = en(nc.sbuf_tensor("pt_stage", [128, 1024], BF16))
            dq_pk = en(nc.sbuf_tensor("dq_pk", [128, 704], I32))
            dq_nib = en(nc.sbuf_tensor("dq_nib", [128, 1024], I32))
            dq_tmp = en(nc.sbuf_tensor("dq_tmp", [128, 1024], I32))
            dq_msk = en(nc.sbuf_tensor("dq_msk", [128, 1024], I32))
            dq_fix = en(nc.sbuf_tensor("dq_fix", [128, 1024], F32))
            dq_s16 = en(nc.sbuf_tensor("dq_s16", [128, 44], F16))
            dq_sc = en(nc.sbuf_tensor("dq_sc", [128, 44], F32))
            dq_out = en(nc.sbuf_tensor("dq_out", [128, 1024], BF16))
            pt_ps = [en(nc.psum_tensor(_uname("ps"), [128, 1024], BF16)) for _i in range(2)]

            for name, src in (("wq", wq_in), ("wk", wk_in), ("wv", wv_in),
                              ("wo", wo_in)):
                dma(gp, agw[name][0].ap(), src)
            p.wait(gp, "g")
            for name in ("wq", "wk", "wv", "wo"):
                ag_sems[name] = collective("AllGather", [agw[name][0].ap().opt()],
                                           [agw[name][1].ap().opt()])

            def prep_transpose(src_ap, rows, cols, dst_ap, dst_row_off=0):
                nrt = rows // 128
                nct = cols // 128
                for c in range(nct):
                    for r0 in range(0, nrt, 8):
                        nblk = min(8, nrt - r0)
                        pb = pt_ps[(r0 // 8) % 2]
                        for b in range(nblk):
                            r = r0 + b
                            p.wait(gp, "t")
                            dma(gp, pt_src[:, 0:cols],
                                src_ap[r * 128:(r + 1) * 128, :])
                            p.wait(te, "g")
                            if b == 0:
                                p.wait(te, "a")
                            Tm(te.transpose(pb[:, b * 128:(b + 1) * 128],
                                            pt_src[:, c * 128:(c + 1) * 128],
                                            iden_bf[:, :]))
                        p.wait(ac, "t")
                        p.wait(ac, "g")
                        A(ac.copy(pt_stage[:, 0:nblk * 128], pb[:, 0:nblk * 128]))
                        p.wait(gp, "a")
                        dma(gp, dst_ap[dst_row_off + c * 128:
                                       dst_row_off + (c + 1) * 128,
                                       r0 * 128:(r0 + nblk) * 128],
                            pt_stage[:, 0:nblk * 128])
                p.wait(gp, "g")

            prep_transpose(w_in_in, PDIM - 128, 256, agw["w_in"][0].ap())
            p.wait(gp, "t")
            dma(gp, pt_src[:, 0:256], w_in_in[PDIM - 128:PDIM, :])
            p.wait(te, "g")
            p.wait(te, "a")
            for c in range(2):
                inst = te.transpose(pt_ps[0][:, c * 128:(c + 1) * 128],
                                    pt_src[:, c * 128:(c + 1) * 128],
                                    iden_bf[:, :])
                p.inc(inst, "t")
            p.wait(ac, "t")
            p.wait(ac, "g")
            A(ac.copy(pt_stage[:, 0:256], pt_ps[0][:, 0:256]))
            p.wait(gp, "a")
            for c in range(2):
                dma(gp, agw["w_in"][0].ap()[c * 128:(c + 1) * 128,
                                            PDIM - 128:PDIM],
                    pt_stage[:, c * 128:(c + 1) * 128])
            p.wait(gp, "g")
            ag_sems["w_in"] = collective("AllGather", [agw["w_in"][0].ap().opt()],
                                         [agw["w_in"][1].ap().opt()])
            prep_transpose(outp_in, E, 512, agw["outp"][0].ap())
            ag_sems["outp"] = collective("AllGather", [agw["outp"][0].ap().opt()],
                                         [agw["outp"][1].ap().opt()])
            prep_transpose(gate_in, INTER, 256, agw["gate"][0].ap())
            ag_sems["gate"] = collective("AllGather", [agw["gate"][0].ap().opt()],
                                         [agw["gate"][1].ap().opt()])
            prep_transpose(up_in, INTER, 256, agw["up"][0].ap())
            ag_sems["up"] = collective("AllGather", [agw["up"][0].ap().opt()],
                                       [agw["up"][1].ap().opt()])
            for cpass in range(4):
                prep_transpose(down_in[:, cpass * 256:(cpass + 1) * 256], E, 256,
                               agw["down"][0].ap(), dst_row_off=cpass * 256)
            ag_sems["down"] = collective("AllGather", [agw["down"][0].ap().opt()],
                                         [agw["down"][1].ap().opt()])

            def dequant_tile(words_view, incols, sc_off):
                for j in range(8):
                    inst = ve.tensor_scalar(dq_nib.ap()[:, j:incols:8],
                                            words_view, 4 * j, 15,
                                            op0=ALU.logical_shift_right,
                                            op1=ALU.bitwise_and)
                    p.inc(inst, "v")
                p.wait(ve, "v")
                V(ve.tensor_scalar(dq_tmp[:, 0:incols], dq_nib[:, 0:incols],
                                   7, None, op0=ALU.bitwise_and))
                V(ve.tensor_scalar(dq_msk[:, 0:incols], dq_tmp[:, 0:incols],
                                   2, None, op0=ALU.is_lt))
                V(ve.tensor_scalar(dq_tmp[:, 0:incols], dq_tmp[:, 0:incols],
                                   252, None, op0=ALU.add))
                V(ve.tensor_scalar(dq_tmp[:, 0:incols], dq_tmp[:, 0:incols],
                                   22, None, op0=ALU.logical_shift_left))
                V(ve.tensor_scalar(dq_fix[:, 0:incols],
                                   dq_tmp.ap().bitcast(F32)[:, 0:incols],
                                   2.0, 1.0, op0=ALU.mult, op1=ALU.subtract))
                V(ve.select(dq_tmp.ap().bitcast(F32)[:, 0:incols],
                            dq_msk[:, 0:incols], dq_fix[:, 0:incols],
                            dq_tmp.ap().bitcast(F32)[:, 0:incols]))
                V(ve.tensor_scalar(dq_nib[:, 0:incols], dq_nib[:, 0:incols],
                                   8, 28, op0=ALU.bitwise_and,
                                   op1=ALU.logical_shift_left))
                V(ve.tensor_tensor(dq_tmp[:, 0:incols], dq_tmp[:, 0:incols],
                                   dq_nib[:, 0:incols], op=ALU.bitwise_or))
                p.wait(ve, "t")
                for gq in range(incols // 128):
                    inst = ve.tensor_scalar_mul(
                        dq_out[:, gq * 128:(gq + 1) * 128],
                        dq_tmp.ap().bitcast(F32)[:, gq * 128:(gq + 1) * 128],
                        dq_sc[:, sc_off + gq:sc_off + gq + 1])
                    p.inc(inst, "v")
                p.wait(ve, "v")

            for r in range(2 * IQ // 128):
                p.wait(gp, "v")
                p.wait(gp, "a")
                dma(gp, dq_pk[:, 0:32], gup_in[r * 128:(r + 1) * 128, :])
                dma(gp, dq_s16[:, 0:2], gus_in[r * 128:(r + 1) * 128, :])
                p.wait(ve, "g")
                V(ve.tensor_copy(dq_sc[:, 0:2], dq_s16[:, 0:2]))
                dequant_tile(dq_pk[:, 0:32], 256, 0)
                pb = pt_ps[r % 2]
                p.wait(te, "v")
                p.wait(te, "a")
                for c in range(2):
                    inst = te.transpose(pb[:, c * 128:(c + 1) * 128],
                                        dq_out[:, c * 128:(c + 1) * 128],
                                        iden_bf[:, :])
                    p.inc(inst, "t")
                p.wait(ac, "t")
                p.wait(ac, "g")
                A(ac.copy(pt_stage[:, 0:256], pb[:, 0:256]))
                p.wait(gp, "a")
                for c in range(2):
                    dma(gp, agw["wgu"][0].ap()[c * 128:(c + 1) * 128,
                                               r * 128:(r + 1) * 128],
                        pt_stage[:, c * 128:(c + 1) * 128])
            p.wait(gp, "g")
            ag_sems["wgu"] = collective("AllGather", [agw["wgu"][0].ap().opt()],
                                        [agw["wgu"][1].ap().opt()])

            for r in range(2):
                p.wait(gp, "v")
                p.wait(gp, "a")
                dma(gp, dq_pk[:, 0:704], dnp_in[r * 128:(r + 1) * 128, :])
                dma(gp, dq_s16[:, 0:44], dns_in[r * 128:(r + 1) * 128, :])
                p.wait(ve, "g")
                V(ve.tensor_copy(dq_sc[:, 0:44], dq_s16[:, 0:44]))
                for seg in range(6):
                    incols = 1024 if seg < 5 else 512
                    nblk = incols // 128
                    dequant_tile(dq_pk[:, seg * 128:seg * 128 + incols // 8],
                                 incols, seg * 8)
                    pb = pt_ps[seg % 2]
                    p.wait(te, "v")
                    p.wait(te, "a")
                    for c in range(nblk):
                        inst = te.transpose(pb[:, c * 128:(c + 1) * 128],
                                            dq_out[:, c * 128:(c + 1) * 128],
                                            iden_bf[:, :])
                        p.inc(inst, "t")
                    p.wait(ac, "t")
                    p.wait(ac, "g")
                    A(ac.copy(pt_stage[:, 0:incols], pb[:, 0:incols]))
                    p.wait(gp, "a")
                    for c in range(nblk):
                        dma(gp, agw["wdn"][0].ap()[seg * 1024 + c * 128:
                                                   seg * 1024 + (c + 1) * 128,
                                                   r * 128:(r + 1) * 128],
                            pt_stage[:, c * 128:(c + 1) * 128])
            p.wait(gp, "g")
            ag_sems["wdn"] = collective("AllGather", [agw["wdn"][0].ap().opt()],
                                        [agw["wdn"][1].ap().opt()])

            for c in range(4):
                p.wait(gp, "t")
                dma(gp, pt_src[:, 0:128], dtw_in[c * 128:(c + 1) * 128, :])
                p.wait(te, "g")
                p.wait(te, "a")
                Tm(te.transpose(pt_ps[0][:, 0:128], pt_src[:, 0:128],
                                iden_bf[:, :]))
                p.wait(ac, "t")
                A(ac.copy(dtwT[:, c * 128:(c + 1) * 128], pt_ps[0][:, 0:128]))
            p.wait(gp, "a")

        if "wdnchk" in tapd:
            gp.wait_ge(ag_sems["wdn"], 1)
            dma(gp, tapd["wdnchk"], agw["wdn"][1].ap()[0:IQ, :])
            p.wait(gp, "g")
        if "wguchk" in tapd:
            gp.wait_ge(ag_sems["wgu"], 1)
            dma(gp, tapd["wguchk"], agw["wgu"][1].ap()[0:256, :])
            p.wait(gp, "g")
        fence()
        if STOP_AFTER == "prep":
            with ExitStack() as ph:
                x_sb = ph.enter_context(nc.sbuf_tensor("x_sb", [128, 8 * E], F32))
                shared["x"] = x_sb
                for sem in ag_sems.values():
                    gp.wait_ge(sem, 1)
                V(ve.memset(x_sb[:, :], 0.0))
                p.wait(sy, "v")
                final_output_and_done()
            return nc

        with ExitStack() as phx:
            x_sb = phx.enter_context(nc.sbuf_tensor("x_sb", [128, 8 * E], F32))
            shared["x"] = x_sb
            for j in range(8):
                dma(sy, x_sb[:, j * E:(j + 1) * E],
                    x0_in[j * 128:(j + 1) * 128, :], "s")

            if "xin" in tapd:
                p.wait(gp, "v"); p.wait(gp, "a"); p.wait(gp, "t")
                p.wait(gp, "s")
                for j in range(8):
                    dma(gp, tapd["xin"][j * 128:(j + 1) * 128, :],
                        x_sb[:, j * E:(j + 1) * E])
                p.wait(gp, "g")
            with ExitStack() as ph:
                en = ph.enter_context
                hT = en(nc.sbuf_tensor("hT", [128, 16 * TL], BF16))
                wfull = en(nc.sbuf_tensor("wfull", [128, 16 * E], BF16))
                tr_psl = [en(nc.psum_tensor(_uname("ps"), [128, 1024], BF16)) for _i in range(2)]
                qk_ps = [en(nc.psum_tensor(_uname("ps"), [128, 512], F32)) for _i in range(2)]
                shared["tr_ps"] = tr_psl

                p.wait(ve, "s")
                p.wait(ac, "s")
                for j in range(8):
                    p.wait(ve, "t")
                    emit_ln(x_sb[:, j * E:(j + 1) * E], h_nat[:, :], True, 1e-5)
                    nat_to_T(h_nat[:, :],
                             lambda c, j=j: hT[:, c * TL + j * 128:
                                               c * TL + (j + 1) * 128])
                if "ln0T" in tapd:
                    p.wait(gp, "a"); p.wait(gp, "v"); p.wait(gp, "t")
                    dma(gp, tapd["ln0T"], hT[:, :])
                    p.wait(gp, "g")

                def load_wfull(name):
                    p.wait(sy, "t")
                    dma(sy, wfull.ap().rearrange("p (c n) -> p c n", c=16),
                        agw[name][1].ap().rearrange("(c p) n -> p c n", p=128),
                        "s")

                def proj_qk(qk_idx):
                    for dq in range(16):
                        p.wait(ac, "g")
                        for tc in range(2):
                            pb = qk_ps[tc]
                            p.wait(te, "s")
                            p.wait(te, "a")
                            for c in range(16):
                                inst = te.matmul(
                                    pb[:, :],
                                    wfull[:, c * E + dq * 128:
                                          c * E + (dq + 1) * 128],
                                    hT[:, c * TL + tc * 512:
                                       c * TL + (tc + 1) * 512],
                                    start=(c == 0), stop=(c == 15))
                            p.inc(inst, "t")
                            p.wait(ac, "t")
                            A(ac.copy(stg[:, tc * 512:(tc + 1) * 512], pb[:, :]))
                        p.wait(gp, "a")
                        j = dq // 2
                        roff = qk_idx * 256 + (dq % 2) * 128
                        dma(gp, a2a_qkv_i.ap()[j * 768 + roff:
                                               j * 768 + roff + 128, :],
                            stg[:, 0:TL])

                sy.wait_ge(ag_sems["wq"], 1)
                load_wfull("wq")
                proj_qk(0)
                sy.wait_ge(ag_sems["wk"], 1)
                load_wfull("wk")
                proj_qk(1)
                sy.wait_ge(ag_sems["wv"], 1)
                load_wfull("wv")
                a2ai_flat = a2a_qkv_i.ap().rearrange("(m r) t -> m (r t)", m=NC)
                for j in range(8):
                    for ec in range(4):
                        pb = qk_ps[ec % 2]
                        p.wait(te, "s")
                        p.wait(te, "a")
                        for c in range(16):
                            inst = te.matmul(
                                pb[:, :],
                                hT[:, c * TL + j * 128:c * TL + (j + 1) * 128],
                                wfull[:, c * E + ec * 512:c * E + (ec + 1) * 512],
                                start=(c == 0), stop=(c == 15))
                        p.inc(inst, "t")
                        p.wait(ac, "t")
                        p.wait(ac, "g")
                        A(ac.copy(stg[:, 0:512], pb[:, :]))
                        p.wait(gp, "a")
                        for half in range(2):
                            jj = ec * 2 + half
                            base = 512 * 1024 + 32768 * j
                            dstv = a2ai_flat[jj:jj + 1, base:base + 32768] \
                                .rearrange("one (pp c) -> (one pp) c", c=256)
                            dma(gp, dstv, stg[:, half * 256:(half + 1) * 256])
                p.wait(gp, "g")
                a2a_qkv_sem = collective("AllToAll", [a2a_qkv_i.ap().opt()],
                                         [a2a_qkv_o.ap().opt()])
                if "qkvT" in tapd:
                    gp.wait_ge(a2a_qkv_sem, 1)
                    dma(gp, tapd["qkvT"], a2a_qkv_o.ap())
                    p.wait(gp, "g")

            fence()
            with ExitStack() as ph:
                en = ph.enter_context
                qt_sb = en(nc.sbuf_tensor("qt_sb", [128, 2 * T], BF16))
                kt_sb = en(nc.sbuf_tensor("kt_sb", [128, 2 * T], BF16))
                v_sb = en(nc.sbuf_tensor("v_sb", [128, 64 * 256], BF16))
                e_sb = en(nc.sbuf_tensor("e_sb", [128, 2048], BF16))
                att_f = en(nc.sbuf_tensor("att_f", [1, 2048], F32))
                att_b = en(nc.sbuf_tensor("att_b", [1, 1024], BF16))
                abc = en(nc.sbuf_tensor("abc", [128, 1024], F32))
                od_sb = en(nc.sbuf_tensor("od_sb", [128, 512], F32))
                t0_sb = en(nc.sbuf_tensor("t0_sb", [128, 512], F32))
                sqv = en(nc.sbuf_tensor("sqv", [128, 512], BF16))
                ofin = en(nc.sbuf_tensor("ofin", [128, 512], BF16))
                sc_ps = [en(nc.psum_tensor(_uname("ps"), [128, 512], F32)) for _i in range(4)]
                o_ps = [en(nc.psum_tensor(_uname("ps"), [128, 512], F32)) for _i in range(2)]
                s_ps = [en(nc.psum_tensor(_uname("ps"), [1, 512], F32)) for _i in range(2)]

                sy.wait_ge(a2a_qkv_sem, 1)
                qkvo_r = a2a_qkv_o.ap().rearrange("(m b pp) t -> pp b m t",
                                                  m=NC, b=6, pp=128)
                for b in range(2):
                    dma(sy, qt_sb.ap()[:, b * T:(b + 1) * T].rearrange(
                            "p (m t) -> p m t", m=8),
                        qkvo_r[:, b, :, :], "s")
                    dma(sy, kt_sb.ap()[:, b * T:(b + 1) * T].rearrange(
                            "p (m t) -> p m t", m=8),
                        qkvo_r[:, 2 + b, :, :], "s")
                a2ao_flat = a2a_qkv_o.ap().rearrange("(m r) t -> m (r t)", m=NC)
                vsrc = a2ao_flat[:, 512 * 1024:768 * 1024].rearrange(
                    "m (t8 pp c) -> pp m t8 c", t8=8, pp=128, c=256)
                for m in range(8):
                    dma(sy, v_sb.ap().rearrange("p (m t8 c) -> p m t8 c",
                                                m=8, t8=8)[:, m, :, :],
                        vsrc[:, m, :, :], "s")
                p.wait(te, "s")

                exp_done = {}
                attnv_done = {}
                for dh in range(2):
                    for qt in range(16):
                        nkt = 4 * qt + 4
                        for kt in range(nkt):
                            bi = kt % 2
                            if (dh, qt, kt - 2) in exp_done:
                                te.wait_ge(p.sem["a"],
                                           exp_done[(dh, qt, kt - 2)])
                            elif kt == 0 and (dh, qt) != (0, 0):
                                te.wait_ge(p.sem["v"], p.cnt["v"])
                            te.matmul(sc_ps[2 * bi][:, :],
                                      kt_sb[0:64, dh * T + kt * 128:
                                            dh * T + (kt + 1) * 128],
                                      qt_sb[0:64, dh * T + qt * 512:
                                            dh * T + (qt + 1) * 512],
                                      start=True, stop=True,
                                      tile_position=(0, 0))
                            inst = te.matmul(sc_ps[2 * bi + 1][:, :],
                                             kt_sb[64:128, dh * T + kt * 128:
                                                   dh * T + (kt + 1) * 128],
                                             qt_sb[64:128, dh * T + qt * 512:
                                                   dh * T + (qt + 1) * 512],
                                             start=True, stop=True,
                                             tile_position=(64, 0))
                            p.inc(inst, "t")
                            ac.wait_ge(p.sem["t"], p.cnt["t"])
                            if (dh, qt, kt - 2) in attnv_done:
                                ac.wait_ge(p.sem["t"],
                                           attnv_done[(dh, qt, kt - 2)])
                            ac.activation(e_sb[:, (2 * bi) * 512:
                                               (2 * bi + 1) * 512],
                                          sc_ps[2 * bi][:, :], AF.Exp,
                                          bias=attc[:, 0:1], scale=attc[:, 1:2])
                            inst = ac.activation(
                                e_sb[:, (2 * bi + 1) * 512:(2 * bi + 2) * 512],
                                sc_ps[2 * bi + 1][:, :], AF.Exp,
                                bias=attc[:, 0:1], scale=attc[:, 1:2])
                            p.inc(inst, "a", 2)
                            exp_done[(dh, qt, kt)] = p.cnt["a"]
                            diag = kt >= 4 * qt
                            if diag:
                                gp.wait_ge(p.sem["a"], exp_done[(dh, qt, kt)])
                                gp.affine_select(
                                    e_sb[:, (2 * bi) * 512:(2 * bi + 1) * 512],
                                    e_sb[:, (2 * bi) * 512:(2 * bi + 1) * 512],
                                    [[1, 512]], compare_op=ALU.is_ge, fill=0.0,
                                    base=512 * qt - 128 * kt,
                                    channel_multiplier=-1)
                                inst = gp.affine_select(
                                    e_sb[:, (2 * bi + 1) * 512:
                                         (2 * bi + 2) * 512],
                                    e_sb[:, (2 * bi + 1) * 512:
                                         (2 * bi + 2) * 512],
                                    [[1, 512]], compare_op=ALU.is_ge, fill=0.0,
                                    base=512 * qt - 128 * kt,
                                    channel_multiplier=-1)
                                p.inc(inst, "q", 2)
                                te.wait_ge(p.sem["q"], p.cnt["q"])
                            else:
                                te.wait_ge(p.sem["a"], exp_done[(dh, qt, kt)])
                            e0 = e_sb[:, (2 * bi) * 512:(2 * bi + 1) * 512]
                            e1 = e_sb[:, (2 * bi + 1) * 512:(2 * bi + 2) * 512]
                            vt = v_sb[:, kt * 256 + dh * 128:
                                      kt * 256 + (dh + 1) * 128]
                            st = (kt == 0)
                            sp = (kt == nkt - 1)
                            te.matmul(o_ps[0][:, :], vt, e0, start=st, stop=sp)
                            te.matmul(o_ps[1][:, :], vt, e1, start=st, stop=sp)
                            te.matmul(s_ps[0][0:1, :], ones_col[:, :], e0,
                                      start=st, stop=sp)
                            inst = te.matmul(s_ps[1][0:1, :], ones_col[:, :], e1,
                                             start=st, stop=sp)
                            p.inc(inst, "t")
                            attnv_done[(dh, qt, kt)] = p.cnt["t"]

                        p.wait(ve, "t")
                        V(ve.reciprocal(att_f[0:1, 0:512], s_ps[0][0:1, :]))
                        V(ve.reciprocal(att_f[0:1, 512:1024], s_ps[1][0:1, :]))
                        V(ve.tensor_scalar_mul(att_f[0:1, 512:1024],
                                               att_f[0:1, 512:1024],
                                               lam_sb[0:1, 0:1]))
                        V(ve.tensor_copy(att_b[0:1, 0:512], att_f[0:1, 0:512]))
                        V(ve.tensor_copy(att_b[0:1, 512:1024],
                                         att_f[0:1, 512:1024]))
                        p.wait(te, "v")
                        te.matmul(sc_ps[0][:, :], ones_row[0:1, :],
                                  att_b[0:1, 0:512], start=True, stop=True)
                        inst = te.matmul(sc_ps[1][:, :], ones_row[0:1, :],
                                         att_b[0:1, 512:1024],
                                         start=True, stop=True)
                        p.inc(inst, "t")
                        p.wait(ac, "t")
                        ac.copy(abc[:, 0:512], sc_ps[0][:, :])
                        A(ac.copy(abc[:, 512:1024], sc_ps[1][:, :]))
                        p.wait(ve, "a")
                        V(ve.tensor_tensor(t0_sb[:, :], o_ps[0][:, :],
                                           abc[:, 0:512], op=ALU.mult))
                        V(ve.tensor_tensor(od_sb[:, :], o_ps[1][:, :],
                                           abc[:, 512:1024], op=ALU.mult))
                        V(ve.tensor_tensor(od_sb[:, :], t0_sb[:, :],
                                           od_sb[:, :], op=ALU.subtract))
                        p.wait(ac, "v")
                        A(ac.activation(sqv[:, :], od_sb[:, :], AF.Square))
                        p.wait(te, "a")
                        inst = te.matmul(s_ps[0][0:1, :], ones_col[:, :],
                                         sqv[:, :], start=True, stop=True)
                        p.inc(inst, "t")
                        p.wait(ve, "t")
                        V(ve.tensor_scalar(att_f[0:1, 1024:1536],
                                           s_ps[0][0:1, :], 1.0 / 128, 1e-5,
                                           op0=ALU.mult, op1=ALU.add))
                        p.wait(ac, "v")
                        A(ac.sqrt(att_f[0:1, 1536:2048], att_f[0:1, 1024:1536]))
                        p.wait(ve, "a")
                        V(ve.reciprocal(att_f[0:1, 1024:1536],
                                        att_f[0:1, 1536:2048]))
                        V(ve.tensor_scalar_mul(att_b[0:1, 0:512],
                                               att_f[0:1, 1024:1536],
                                               1.0 - LAM_INIT))
                        p.wait(te, "v")
                        inst = te.matmul(sc_ps[2][:, :], ones_row[0:1, :],
                                         att_b[0:1, 0:512],
                                         start=True, stop=True)
                        p.inc(inst, "t")
                        p.wait(ac, "t")
                        A(ac.copy(abc[:, 0:512], sc_ps[2][:, :]))
                        p.wait(ve, "a")
                        V(ve.tensor_tensor(ofin[:, :], od_sb[:, :],
                                           abc[:, 0:512], op=ALU.mult))
                        p.wait(gp, "v")
                        j = qt // 2
                        dma(gp, a2a_o_i.ap()[j * 256 + dh * 128:
                                             j * 256 + (dh + 1) * 128,
                                             (qt % 2) * 512:(qt % 2) * 512 + 512],
                            ofin[:, :])
                p.wait(gp, "g")
                a2a_o_sem = collective("AllToAll", [a2a_o_i.ap().opt()],
                                       [a2a_o_o.ap().opt()])

            fence()
            with ExitStack() as ph:
                en = ph.enter_context
                otT = en(nc.sbuf_tensor("otT", [128, 16 * TL], BF16))
                wo_sb = en(nc.sbuf_tensor("wo_sb", [128, 16 * E], BF16))
                wo_ps = [en(nc.psum_tensor(_uname("ps"), [128, 512], F32)) for _i in range(2)]
                if "attO" in tapd:
                    gp.wait_ge(a2a_o_sem, 1)
                    dma(gp, tapd["attO"], a2a_o_o.ap())
                    p.wait(gp, "g")
                sy.wait_ge(a2a_o_sem, 1)
                dma(sy, otT.ap().rearrange("p (c t) -> p c t", c=16),
                    a2a_o_o.ap().rearrange("(c p) t -> p c t", p=128), "s")
                sy.wait_ge(ag_sems["wo"], 1)
                dma(sy, wo_sb.ap().rearrange("p (c n) -> p c n", c=16),
                    agw["wo"][1].ap().rearrange("(c p) n -> p c n", p=128), "s")
                p.wait(te, "s")
                for tt in range(8):
                    for ec in range(4):
                        pb = wo_ps[ec % 2]
                        p.wait(te, "v")
                        for c in range(16):
                            inst = te.matmul(
                                pb[:, :],
                                otT[:, c * TL + tt * 128:c * TL + (tt + 1) * 128],
                                wo_sb[:, c * E + ec * 512:c * E + (ec + 1) * 512],
                                start=(c == 0), stop=(c == 15))
                        p.inc(inst, "t")
                        p.wait(ve, "t")
                        V(ve.tensor_tensor(x_sb[:, tt * E + ec * 512:
                                                tt * E + (ec + 1) * 512],
                                           x_sb[:, tt * E + ec * 512:
                                                tt * E + (ec + 1) * 512],
                                           pb[:, :], op=ALU.add))
                if "x1" in tapd:
                    p.wait(gp, "v"); p.wait(gp, "a"); p.wait(gp, "t")
                    for j in range(8):
                        dma(gp, tapd["x1"][j * 128:(j + 1) * 128, :],
                            x_sb[:, j * E:(j + 1) * E])
                    p.wait(gp, "g")

            fence()
            if STOP_AFTER == "attn":
                final_output_and_done()
                return nc

            def emit_qffn(tag):
                with ExitStack() as ph:
                    en = ph.enter_context
                    h2T = en(nc.sbuf_tensor(f"h2T{tag}", [128, 16 * 512], BF16))
                    hTm = en(nc.sbuf_tensor(f"hTm{tag}", [128, 11 * 512], BF16))
                    wdng = en(nc.sbuf_tensor(f"wdng{tag}", [128, 22528], BF16))
                    y_acc = en(nc.sbuf_tensor(f"y_acc{tag}", [128, 8192], F32))
                    wgu_b = [en(nc.sbuf_tensor(f"wgu{tag}{i}", [128, 4096], BF16))
                             for i in range(2)]
                    sg_st = en(nc.sbuf_tensor(f"sg_st{tag}", [128, 512], BF16))
                    tr_psf = [en(nc.psum_tensor(_uname("ps"), [128, 1024], BF16))
                              for _ in range(2)]
                    gu_ps = [en(nc.psum_tensor(_uname("ps"), [128, 512], F32))
                             for _ in range(2)]
                    dn_ps = [en(nc.psum_tensor(_uname("ps"), [128, 256], F32))
                             for _ in range(2)]
                    shared["tr_ps"] = tr_psf
                    gp.wait_ge(ag_sems["wgu"], 1)
                    gp.wait_ge(ag_sems["wdn"], 1)
                    sy.wait_ge(ag_sems["wgu"], 1)
                    sy.wait_ge(ag_sems["wdn"], 1)
                    for hs in range(2):
                        jt = [4 * hs + k for k in range(4)]
                        for k, j in enumerate(jt):
                            p.wait(ve, "t")
                            emit_ln(x_sb[:, j * E:(j + 1) * E], h_nat[:, :],
                                    True, 1e-5)
                            nat_to_T(h_nat[:, :],
                                     lambda c, k=k: h2T[:, c * 512 + k * 128:
                                                        c * 512 + (k + 1) * 128])
                        if hs == 0 and tag == "a" and "fh2" in tapd:
                            p.wait(gp, "a"); p.wait(gp, "v")
                            dma(gp, tapd["fh2"], h2T[:, :])
                            p.wait(gp, "g")
                        for grp in range(4):
                            for ipx in range(11):
                                ic = grp * 11 + ipx
                                wb = wgu_b[ic % 2]
                                p.wait(sy, "t")
                                dma(sy, wb[:, 0:2048].rearrange(
                                        "p (c w) -> p c w", c=16),
                                    agw["wgu"][1].ap().rearrange(
                                        "(c p) n -> p c n", p=128)
                                    [:, :, ic * 128:(ic + 1) * 128], "s")
                                dma(sy, wb[:, 2048:4096].rearrange(
                                        "p (c w) -> p c w", c=16),
                                    agw["wgu"][1].ap().rearrange(
                                        "(c p) n -> p c n", p=128)
                                    [:, :, IQ + ic * 128:IQ + (ic + 1) * 128],
                                    "s")
                                p.wait(te, "s")
                                p.wait(te, "a")
                                p.wait(te, "v")
                                for c in range(16):
                                    inst = te.matmul(
                                        gu_ps[0][:, :],
                                        wb[:, c * 128:(c + 1) * 128],
                                        h2T[:, c * 512:(c + 1) * 512],
                                        start=(c == 0), stop=(c == 15))
                                p.inc(inst, "t")
                                for c in range(16):
                                    inst = te.matmul(
                                        gu_ps[1][:, :],
                                        wb[:, 2048 + c * 128:
                                           2048 + (c + 1) * 128],
                                        h2T[:, c * 512:(c + 1) * 512],
                                        start=(c == 0), stop=(c == 15))
                                p.inc(inst, "t")
                                p.wait(ac, "t")
                                A(ac.activation(sg_st[:, :], gu_ps[0][:, :],
                                                AF.Silu))
                                p.wait(ve, "a")
                                V(ve.tensor_tensor(
                                    hTm[:, ipx * 512:(ipx + 1) * 512],
                                    sg_st[:, :], gu_ps[1][:, :], op=ALU.mult))
                            p.wait(sy, "v")
                            wdnr = agw["wdn"][1].ap().rearrange(
                                "(m icg pp) c -> pp icg m c", m=8, icg=44)
                            for icx in range(11):
                                dma(sy, wdng.ap()[:, icx * 2048:
                                                  (icx + 1) * 2048]
                                    .rearrange("p (m c) -> p m c", m=8),
                                    wdnr[:, grp * 11 + icx, :, :], "s")
                            if grp == 3 and hs == 0 and tag == "a" \
                                    and "wdg" in tapd:
                                p.wait(gp, "s"); p.wait(gp, "v")
                                dma(gp, tapd["wdg"], wdng[:, :])
                                p.wait(gp, "g")
                            for k in range(4):
                                for m8 in range(8):
                                    pb = dn_ps[m8 % 2]
                                    p.wait(te, "s")
                                    p.wait(te, "v")
                                    for ipx in range(11):
                                        inst = te.matmul(
                                            pb[:, :],
                                            hTm[:, ipx * 512 + k * 128:
                                                ipx * 512 + (k + 1) * 128],
                                            wdng[:, (ipx * 8 + m8) * 256:
                                                 (ipx * 8 + m8 + 1) * 256],
                                            start=(ipx == 0), stop=(ipx == 10))
                                    p.inc(inst, "t")
                                    p.wait(ve, "t")
                                    ya = y_acc[:, k * 2048 + m8 * 256:
                                               k * 2048 + (m8 + 1) * 256]
                                    if grp == 0:
                                        V(ve.tensor_copy(ya, pb[:, :]))
                                    else:
                                        V(ve.tensor_tensor(ya, ya, pb[:, :],
                                                           op=ALU.add))
                        if hs == 0 and tag == "a" and "fhm" in tapd:
                            p.wait(gp, "v")
                            dma(gp, tapd["fhm"], hTm[:, :])
                            p.wait(gp, "g")
                        if hs == 0 and tag == "a" and "fya" in tapd:
                            p.wait(gp, "v"); p.wait(gp, "a"); p.wait(gp, "t")
                            dma(gp, tapd["fya"], y_acc[:, :])
                            p.wait(gp, "g")
                        for k, j in enumerate(jt):
                            emit_rstd(y_acc[:, k * 2048:(k + 1) * 2048],
                                      mean_sub=False, eps=1e-6)
                            V(ve.scalar_tensor_tensor(
                                x_sb[:, j * E:(j + 1) * E],
                                y_acc[:, k * 2048:(k + 1) * 2048],
                                col[:, 8:9], x_sb[:, j * E:(j + 1) * E],
                                op0=ALU.mult, op1=ALU.add))

            emit_qffn("a")
            fence()
            if "x2" in tapd:
                p.wait(gp, "v"); p.wait(gp, "a"); p.wait(gp, "t")
                for j in range(8):
                    dma(gp, tapd["x2"][j * 128:(j + 1) * 128, :],
                        x_sb[:, j * E:(j + 1) * E])
                p.wait(gp, "g")
            if STOP_AFTER == "ffn1":
                final_output_and_done()
                return nc

            fence()
            u0_guard = nc.sbuf_tensor("u0", [128, 8 * E], F32, side="right")
            u0 = u0_guard.__enter__()
            with ExitStack() as ph:
                en = ph.enter_context
                n0T = en(nc.sbuf_tensor("n0T", [128, 16 * TL], BF16))
                lnw_bc = en(nc.sbuf_tensor("lnw_bc1", [128, E], BF16))
                wib = [en(nc.sbuf_tensor(f"wib{i}", [128, 2048], BF16))
                       for i in range(2)]
                tr_psf = [en(nc.psum_tensor(_uname("ps"), [128, 1024], BF16))
                          for _ in range(2)]
                ip_ps = [en(nc.psum_tensor(_uname("ps"), [128, 512], F32)) for _i in range(2)]
                bc_ps = en(nc.psum_tensor(_uname("ps"), [128, 512], F32))
                shared["tr_ps"] = tr_psf

                emit_ln_bc(ln1_in, lnw_bc, bc_ps)
                for j in range(8):
                    p.wait(ve, "t")
                    emit_rstd(x_sb[:, j * E:(j + 1) * E], mean_sub=True,
                              eps=1e-5)
                    V(ve.tensor_scalar(u0[:, j * E:(j + 1) * E],
                                       x_sb[:, j * E:(j + 1) * E],
                                       col[:, 2:3], col[:, 8:9],
                                       op0=ALU.subtract, op1=ALU.mult))
                    emit_rstd(u0[:, j * E:(j + 1) * E], mean_sub=False,
                              eps=1e-5)
                    V(ve.tensor_scalar_mul(h_nat[:, :],
                                           u0[:, j * E:(j + 1) * E],
                                           col[:, 8:9]))
                    V(ve.tensor_tensor(h_nat[:, :], h_nat[:, :], lnw_bc[:, :],
                                       op=ALU.mult))
                    nat_to_T(h_nat[:, :],
                             lambda c, j=j: n0T[:, c * TL + j * 128:
                                                c * TL + (j + 1) * 128])
                for j in range(8):
                    dma(gp, x2_spill.ap()[j * 128:(j + 1) * 128, :],
                        x_sb[:, j * E:(j + 1) * E])

                sy.wait_ge(ag_sems["w_in"], 1)
                for pc in range(81):
                    wb = wib[pc % 2]
                    p.wait(sy, "t")
                    dma(sy, wb.ap().rearrange("p (c w) -> p c w", c=16),
                        agw["w_in"][1].ap().rearrange("(c p) n -> p c n", p=128)
                        [:, :, pc * 128:(pc + 1) * 128], "s")
                    p.wait(ac, "g")
                    for tc in range(2):
                        pb = ip_ps[tc]
                        p.wait(te, "s")
                        p.wait(te, "a")
                        for c in range(16):
                            inst = te.matmul(
                                pb[:, :], wb[:, c * 128:(c + 1) * 128],
                                n0T[:, c * TL + tc * 512:c * TL + (tc + 1) * 512],
                                start=(c == 0), stop=(c == 15))
                        p.inc(inst, "t")
                        p.wait(ac, "t")
                        A(ac.copy(stg[:, tc * 512:(tc + 1) * 512], pb[:, :]))
                    p.wait(gp, "a")
                    if pc < 32:
                        dst = z_spill.ap()[pc * 128:(pc + 1) * 128, :]
                    elif pc < 40:
                        c = pc - 32
                        dst = a2a_sc_i.ap()[c * 768:c * 768 + 128, :]
                    elif pc < 48:
                        c = pc - 40
                        dst = a2a_sc_i.ap()[c * 768 + 128:c * 768 + 256, :]
                    elif pc < 80:
                        c = pc - 48
                        j2 = c // 4
                        dst = a2a_sc_i.ap()[j2 * 768 + 256 + (c % 4) * 128:
                                            j2 * 768 + 256 + (c % 4 + 1) * 128, :]
                    else:
                        dst = ag_dt_i.ap()[0:128, :]
                    dma(gp, dst, stg[:, 0:TL])
                p.wait(gp, "g")
                a2a_sc_sem = collective("AllToAll", [a2a_sc_i.ap().opt()],
                                        [a2a_sc_o.ap().opt()])
                ag_dt_sem = collective("AllGather", [ag_dt_i.ap().opt()],
                                       [ag_dt_o.ap().opt()])
            if "zx" in tapd:
                gp.wait_ge(a2a_sc_sem, 1)
                dma(gp, tapd["zx"], a2a_sc_o.ap())
                p.wait(gp, "g")
            if STOP_AFTER == "inproj":
                final_output_and_done()
                u0_guard.__exit__(None, None, None)
                return nc

        fence()
        with ExitStack() as ph:
            en = ph.enter_context
            dt_sl = en(nc.sbuf_tensor("dt_sl", [128, 8192], BF16))
            xc_sl = en(nc.sbuf_tensor("xc_sl", [128, 8192], BF16))
            u_c = en(nc.sbuf_tensor("u_c", [128, 2048], BF16))
            bt_bufs = [en(nc.sbuf_tensor(f"bt{i}", [128, 2048], BF16))
                       for i in range(2)]
            ct_bufs = [en(nc.sbuf_tensor(f"ct{i}", [128, 2048], BF16))
                       for i in range(2)]
            scr4 = en(nc.sbuf_tensor("scr4", [128, 8192], F32))
            y_acc4 = en(nc.sbuf_tensor("y_acc4", [128, 8192], F32))
            dtall = en(nc.sbuf_tensor("dtall", [128, 2048], BF16))
            xrep = en(nc.sbuf_tensor("xrep", [128, 2052], BF16))
            acol = en(nc.sbuf_tensor("acol", [128, 64], F32))
            dtb_sb = en(nc.sbuf_tensor("dtb_sb", [128, 4], F32))
            cvw_sb = en(nc.sbuf_tensor("cvw_sb", [128, 16], F32))
            cvb_sb = en(nc.sbuf_tensor("cvb_sb", [128, 4], F32))
            dpar_sb = en(nc.sbuf_tensor("dpar_sb", [128, 4], F32))
            carry = en(nc.sbuf_tensor("carry", [128, 64], F32))
            dt_ps = en(nc.psum_tensor(_uname("ps"), [128, 512], F32))

            arg_f = scr4[:, 0:2048]
            a_f = scr4[:, 2048:4096]
            b_f = scr4[:, 4096:6144]
            h_f = scr4[:, 6144:8192]
            spf = scr4[:, 0:1536]
            sprl = scr4[:, 1536:2048]
            y_acc = [y_acc4[:, c * 2048:(c + 1) * 2048] for c in range(4)]

            dma(gp, acol[:, :], alog_in)
            dma(gp, dtb_sb[:, :], dtb_in)
            dma(gp, cvw_sb[:, :], convw_in)
            dma(gp, cvb_sb[:, :], convb_in)
            dma(gp, dpar_sb[:, :], dpar_in)
            p.wait(ac, "g")
            A(ac.activation(acol[:, :], acol[:, :], AF.Exp))
            p.wait(ve, "a")
            V(ve.tensor_scalar_mul(acol[:, :], acol[:, :], -1.0))
            V(ve.memset(carry[:, :], 0.0))
            for c in range(4):
                V(ve.memset(y_acc[c], 0.0))

            gp.wait_ge(a2a_sc_sem, 1)
            gp.wait_ge(ag_dt_sem, 1)
            sy.wait_ge(a2a_sc_sem, 1)
            sy.wait_ge(ag_dt_sem, 1)
            hc_done = {}
            sc_o = a2a_sc_o.ap()

            for cc in range(4):
                p.wait(sy, "t")
                dma(sy, dtall.ap().rearrange("p (m t) -> p m t", m=2),
                    ag_dt_o.ap().rearrange("(m p) t -> p m t", p=128)
                    [:, 2 * cc:2 * cc + 2, :], "s")
                for dc in range(4):
                    for tc in range(4):
                        p.wait(te, "s")
                        p.wait(te, "v")
                        p.wait(te, "a")
                        inst = te.matmul(dt_ps[:, :],
                                         dtwT[:, dc * 128:(dc + 1) * 128],
                                         dtall[:, tc * 512:(tc + 1) * 512],
                                         start=True, stop=True)
                        p.inc(inst, "t")
                        p.wait(ac, "t")
                        A(ac.activation(spf[:, 0:512], dt_ps[:, :], AF.Abs,
                                        bias=dtb_sb[:, dc:dc + 1]))
                        A(ac.activation(spf[:, 512:1024], spf[:, 0:512],
                                        AF.Exp, scale=-1.0))
                        A(ac.activation(spf[:, 1024:1536], spf[:, 512:1024],
                                        AF.Ln, bias=1.0))
                        p.wait(ve, "a")
                        V(ve.tensor_scalar(sprl[:, :], dt_ps[:, :],
                                           dtb_sb[:, dc:dc + 1], 0.0,
                                           op0=ALU.add, op1=ALU.max))
                        V(ve.tensor_tensor(
                            dt_sl[:, dc * 2048 + tc * 512:
                                  dc * 2048 + (tc + 1) * 512],
                            spf[:, 1024:1536], sprl[:, :], op=ALU.add))
                for qd in range(4):
                    p.wait(gp, "v")
                    if cc == 0:
                        V(ve.memset(xrep[:, 0:3], 0.0))
                    else:
                        m0 = 2 * cc - 1
                        for av in range(2):
                            for bv in range(4):
                                dma(gp, xrep[av * 64 + bv * 16:
                                             av * 64 + bv * 16 + 16, 0:3],
                                    sc_o[768 * m0 + 32 * qd + 16 * av:
                                         768 * m0 + 32 * qd + 16 * av + 16,
                                         TL - 3:TL])
                    for mi in range(2):
                        m = 2 * cc + mi
                        for av in range(2):
                            for bv in range(4):
                                dma(gp, xrep[av * 64 + bv * 16:
                                             av * 64 + bv * 16 + 16,
                                             3 + mi * TL:3 + (mi + 1) * TL],
                                    sc_o[768 * m + 32 * qd + 16 * av:
                                         768 * m + 32 * qd + 16 * av + 16,
                                         0:TL])
                    p.wait(ve, "g")
                    V(ve.tensor_scalar_mul(arg_f[:, :], xrep[:, 0:2048],
                                           cvw_sb[:, qd * 4:qd * 4 + 1]))
                    for jj in range(1, 4):
                        V(ve.scalar_tensor_tensor(
                            arg_f[:, :], xrep[:, jj:jj + 2048],
                            cvw_sb[:, qd * 4 + jj:qd * 4 + jj + 1],
                            arg_f[:, :], op0=ALU.mult, op1=ALU.add))
                    p.wait(ac, "v")
                    A(ac.activation(xc_sl[:, qd * 2048:(qd + 1) * 2048],
                                    arg_f[:, :], AF.Silu,
                                    bias=cvb_sb[:, qd:qd + 1]))
                p.wait(ve, "a")
                for c in range(4):
                    V(ve.tensor_mul(u_c[:, :],
                                    dt_sl[:, c * 2048:(c + 1) * 2048],
                                    xc_sl[:, c * 2048:(c + 1) * 2048]))
                    for n in range(16):
                        it = cc * 64 + c * 16 + n
                        bi = it % 2
                        if it - 2 in hc_done:
                            gp.wait_ge(p.sem["v"], hc_done[it - 2])
                            sy.wait_ge(p.sem["v"], hc_done[it - 2])
                        bsv = _bsrc(sc_o, cc, c, n)
                        csv = _csrc(sc_o, cc, c, n)
                        for av in range(2):
                            dma(gp, bt_bufs[bi][av * 64:(av + 1) * 64, :]
                                .rearrange("p (m t) -> p m t", m=2),
                                bsv[av, :, :, :])
                        for av in range(8):
                            dma(sy, ct_bufs[bi][av * 16:(av + 1) * 16, :]
                                .rearrange("p (m t) -> p m t", m=2),
                                csv[av, :, :, :], "s")
                        V(ve.tensor_scalar_mul(arg_f[:, :],
                                               dt_sl[:, c * 2048:(c + 1) * 2048],
                                               acol[:, c * 16 + n:
                                                    c * 16 + n + 1]))
                        p.wait(ac, "v")
                        A(ac.activation(a_f[:, :], arg_f[:, :], AF.Exp))
                        p.wait(ve, "g")
                        p.wait(ve, "s")
                        p.wait(ve, "q")
                        p.wait(ve, "a")
                        V(ve.tensor_tensor(b_f[:, :], u_c[:, :], bt_bufs[bi][:, :],
                                           op=ALU.mult))
                        V(ve.tensor_tensor_scan(h_f[:, :], a_f[:, :], b_f[:, :],
                                                carry[:, c * 16 + n:
                                                      c * 16 + n + 1],
                                                op0=ALU.mult, op1=ALU.add))
                        V(ve.tensor_copy(carry[:, c * 16 + n:c * 16 + n + 1],
                                         h_f[:, 2047:2048]))
                        V(ve.tensor_tensor(b_f[:, :], h_f[:, :], ct_bufs[bi][:, :],
                                           op=ALU.mult))
                        hc_done[it] = p.cnt["v"]
                        gp.wait_ge(p.sem["v"], p.cnt["v"])
                        Q(gp.tensor_add(y_acc[c], y_acc[c], b_f[:, :]))
                for c in range(4):
                    p.wait(ve, "q")
                    V(ve.scalar_tensor_tensor(
                        y_acc[c], xc_sl[:, c * 2048:(c + 1) * 2048],
                        dpar_sb[:, c:c + 1], y_acc[c],
                        op0=ALU.mult, op1=ALU.add))
                    p.wait(gp, "v")
                    for mi in range(2):
                        j2 = 2 * cc + mi
                        dma(gp, a2a_y_i.ap()[j2 * DSL + c * 128:
                                             j2 * DSL + (c + 1) * 128, :],
                            y_acc[c][:, mi * 1024:(mi + 1) * 1024])
                    p.wait(ve, "g")
                    V(ve.memset(y_acc[c], 0.0))
            p.wait(gp, "g")
            a2a_y_sem = collective("AllToAll", [a2a_y_i.ap().opt()],
                                   [a2a_y_o.ap().opt()])
        if "ys" in tapd:
            gp.wait_ge(a2a_y_sem, 1)
            dma(gp, tapd["ys"], a2a_y_o.ap())
            p.wait(gp, "g")
        if STOP_AFTER == "scan":
            fence()
            with ExitStack() as ph:
                x_sb = ph.enter_context(nc.sbuf_tensor("x_sb2",
                                                       [128, 8 * E], F32))
                shared["x"] = x_sb
                for j in range(8):
                    dma(sy, x_sb[:, j * E:(j + 1) * E],
                        x2_spill.ap()[j * 128:(j + 1) * 128, :], "s")
                p.wait(sy, "s")
                final_output_and_done()
            u0_guard.__exit__(None, None, None)
            return nc

        fence()
        with ExitStack() as phx:
            with ExitStack() as ph:
                en = ph.enter_context
                yT = en(nc.sbuf_tensor("yT", [128, 32 * 1024], BF16))
                wop_b = [en(nc.sbuf_tensor(f"wop{i}", [128, 2048], BF16))
                         for i in range(2)]
                zst = en(nc.sbuf_tensor("zst", [128, 1024], BF16))
                zsl = en(nc.sbuf_tensor("zsl", [128, 1024], BF16))
                op_ps = [en(nc.psum_tensor(_uname("ps"), [128, 512], F32)) for _i in range(4)]

                gp.wait_ge(a2a_y_sem, 1)
                ysrc = a2a_y_o.ap().rearrange("(m f p) t -> p m f t", m=8, f=4)
                for m in range(8):
                    dma(gp, yT.ap().rearrange("p (m f t) -> p m f t",
                                              m=8, f=4)[:, m, :, :],
                        ysrc[:, m, :, :])
                p.wait(gp, "g")
                for d32 in range(32):
                    p.wait(gp, "a")
                    dma(gp, zst[:, :],
                        z_spill.ap()[d32 * 128:(d32 + 1) * 128, :])
                    p.wait(ac, "g")
                    A(ac.activation(zsl[:, :], zst[:, :], AF.Silu))
                    p.wait(ve, "a")
                    V(ve.tensor_tensor(yT[:, d32 * 1024:(d32 + 1) * 1024],
                                       yT[:, d32 * 1024:(d32 + 1) * 1024],
                                       zsl[:, :], op=ALU.mult))
                sy.wait_ge(ag_sems["outp"], 1)
                p.wait(sy, "v")
                tcount = {}
                for tt in range(8):
                    for d32 in range(32):
                        wb = wop_b[d32 % 2]
                        if (tt, d32 - 2) in tcount:
                            sy.wait_ge(p.sem["t"], tcount[(tt, d32 - 2)])
                        elif tt > 0 and d32 < 2:
                            sy.wait_ge(p.sem["t"], p.cnt["t"])
                        dma(sy, wb[:, :],
                            agw["outp"][1].ap()[d32 * 128:(d32 + 1) * 128, :],
                            "s")
                        p.wait(te, "s")
                        if d32 == 0:
                            p.wait(te, "v")
                        for ec in range(4):
                            inst = te.matmul(
                                op_ps[ec][:, :],
                                yT[:, d32 * 1024 + tt * 128:
                                   d32 * 1024 + (tt + 1) * 128],
                                wb[:, ec * 512:(ec + 1) * 512],
                                start=(d32 == 0), stop=(d32 == 31))
                            if ec == 3:
                                p.inc(inst, "t")
                        tcount[(tt, d32)] = p.cnt["t"]
                    p.wait(ve, "t")
                    for ec in range(4):
                        V(ve.tensor_tensor(u0[:, tt * E + ec * 512:
                                              tt * E + (ec + 1) * 512],
                                           u0[:, tt * E + ec * 512:
                                              tt * E + (ec + 1) * 512],
                                           op_ps[ec][:, :], op=ALU.add))
                if "r1" in tapd:
                    p.wait(gp, "v"); p.wait(gp, "a"); p.wait(gp, "t")
                    for j in range(8):
                        dma(gp, tapd["r1"][j * 128:(j + 1) * 128, :],
                            u0[:, j * E:(j + 1) * E])
                    p.wait(gp, "g")

            fence()
            x_sb = phx.enter_context(nc.sbuf_tensor("x_sb2", [128, 8 * E], F32))
            shared["x"] = x_sb
            for j in range(8):
                dma(sy, x_sb[:, j * E:(j + 1) * E],
                    x2_spill.ap()[j * 128:(j + 1) * 128, :], "s")
            p.wait(ve, "s")
            for j in range(8):
                V(ve.tensor_tensor(x_sb[:, j * E:(j + 1) * E],
                                   x_sb[:, j * E:(j + 1) * E],
                                   u0[:, j * E:(j + 1) * E], op=ALU.add))

            with ExitStack() as ph:
                en = ph.enter_context
                h2T = en(nc.sbuf_tensor("mh2T", [128, 16 * 1024], BF16))
                lnw_bc = en(nc.sbuf_tensor("lnw_bc2", [128, E], BF16))
                tr_psf = [en(nc.psum_tensor(_uname("ps"), [128, 1024], BF16))
                          for _ in range(2)]
                bc_ps = en(nc.psum_tensor(_uname("ps"), [128, 512], F32))
                shared["tr_ps"] = tr_psf
                emit_ln_bc(ln2_in, lnw_bc, bc_ps)

                for j in range(8):
                    hs, k = j // 4, j % 4
                    p.wait(ve, "t")
                    emit_rstd(u0[:, j * E:(j + 1) * E], mean_sub=False,
                              eps=1e-5)
                    V(ve.tensor_scalar_mul(h_nat[:, :],
                                           u0[:, j * E:(j + 1) * E],
                                           col[:, 8:9]))
                    V(ve.tensor_tensor(h_nat[:, :], h_nat[:, :],
                                       lnw_bc[:, :], op=ALU.mult))
                    nat_to_T(h_nat[:, :],
                             lambda c, hs=hs, k=k:
                             h2T[:, c * 1024 + hs * 512 + k * 128:
                                 c * 1024 + hs * 512 + (k + 1) * 128])
                p.wait(gp, "v")
                u0_guard.__exit__(None, None, None)

                fence()
                for hs in range(2):
                    jt = [4 * hs + k for k in range(4)]
                    with ExitStack() as ph2:
                        en2 = ph2.enter_context
                        hTm = en2(nc.sbuf_tensor(f"mhTm{hs}", [128, 8 * 512],
                                                 BF16))
                        dng = en2(nc.sbuf_tensor(f"dng{hs}", [128, 8 * 2048],
                                                 BF16))
                        wgu_b = [en2(nc.sbuf_tensor(f"mwgu{hs}{i}", [128, 4096],
                                                    BF16)) for i in range(2)]
                        sg_st = en2(nc.sbuf_tensor(f"msg_st{hs}", [128, 512],
                                                   BF16))
                        gu_ps = [en2(nc.psum_tensor(_uname("ps"), [128, 512], F32))
                                 for _ in range(2)]
                        dn_ps = [en2(nc.psum_tensor(_uname("ps"), [128, 512], F32))
                                 for _ in range(2)]
                        gp.wait_ge(ag_sems["gate"], 1)
                        gp.wait_ge(ag_sems["up"], 1)
                        gp.wait_ge(ag_sems["down"], 1)
                        sy.wait_ge(ag_sems["gate"], 1)
                        sy.wait_ge(ag_sems["up"], 1)
                        sy.wait_ge(ag_sems["down"], 1)
                        for grp in range(8):
                            for ipx in range(8):
                                ic = grp * 8 + ipx
                                wb = wgu_b[ipx % 2]
                                p.wait(sy, "t")
                                dma(sy, wb[:, 0:2048].rearrange(
                                        "p (c w) -> p c w", c=16),
                                    agw["gate"][1].ap().rearrange(
                                        "(c p) n -> p c n", p=128)
                                    [:, :, ic * 128:(ic + 1) * 128], "s")
                                dma(sy, wb[:, 2048:4096].rearrange(
                                        "p (c w) -> p c w", c=16),
                                    agw["up"][1].ap().rearrange(
                                        "(c p) n -> p c n", p=128)
                                    [:, :, ic * 128:(ic + 1) * 128], "s")
                                p.wait(te, "s")
                                p.wait(te, "a")
                                p.wait(te, "v")
                                for c in range(16):
                                    inst = te.matmul(
                                        gu_ps[0][:, :],
                                        wb[:, c * 128:(c + 1) * 128],
                                        h2T[:, c * 1024 + hs * 512:
                                            c * 1024 + (hs + 1) * 512],
                                        start=(c == 0), stop=(c == 15))
                                p.inc(inst, "t")
                                for c in range(16):
                                    inst = te.matmul(
                                        gu_ps[1][:, :],
                                        wb[:, 2048 + c * 128:
                                           2048 + (c + 1) * 128],
                                        h2T[:, c * 1024 + hs * 512:
                                            c * 1024 + (hs + 1) * 512],
                                        start=(c == 0), stop=(c == 15))
                                p.inc(inst, "t")
                                p.wait(ac, "t")
                                A(ac.activation(sg_st[:, :], gu_ps[0][:, :],
                                                AF.Silu))
                                p.wait(ve, "a")
                                V(ve.tensor_tensor(
                                    hTm[:, ipx * 512:(ipx + 1) * 512],
                                    sg_st[:, :], gu_ps[1][:, :], op=ALU.mult))
                            p.wait(sy, "v")
                            dma(sy, dng.ap().rearrange("p (ic n) -> p ic n",
                                                       ic=8),
                                agw["down"][1].ap().rearrange(
                                    "(ic p) n -> p ic n", p=128)
                                [:, grp * 8:(grp + 1) * 8, :], "s")
                            for k in range(4):
                                for ec in range(4):
                                    pb = dn_ps[ec % 2]
                                    p.wait(te, "s")
                                    p.wait(te, "v")
                                    for ipx in range(8):
                                        inst = te.matmul(
                                            pb[:, :],
                                            hTm[:, ipx * 512 + k * 128:
                                                ipx * 512 + (k + 1) * 128],
                                            dng[:, ipx * 2048 + ec * 512:
                                                ipx * 2048 + (ec + 1) * 512],
                                            start=(ipx == 0), stop=(ipx == 7))
                                    p.inc(inst, "t")
                                    p.wait(ve, "t")
                                    j = jt[k]
                                    V(ve.tensor_tensor(
                                        x_sb[:, j * E + ec * 512:
                                             j * E + (ec + 1) * 512],
                                        x_sb[:, j * E + ec * 512:
                                             j * E + (ec + 1) * 512],
                                        pb[:, :], op=ALU.add))
            if "x3" in tapd:
                p.wait(gp, "v"); p.wait(gp, "a"); p.wait(gp, "t")
                for j in range(8):
                    dma(gp, tapd["x3"][j * 128:(j + 1) * 128, :],
                        x_sb[:, j * E:(j + 1) * E])
                p.wait(gp, "g")
            if STOP_AFTER == "gate":
                final_output_and_done()
                return nc

            fence()
            emit_qffn("b")
            final_output_and_done()
    return nc


def _xsrc(sc_o, m, qd):
    base = sc_o[768 * m + 32 * qd:768 * m + 32 * qd + 32, :]
    v = base.rearrange("(a c) t -> a c t", a=2)
    return v.unsqueeze(1).broadcast_to([2, 4, 16, v.shape[-1]])


def _bsrc(sc_o, cc, c, n):
    rows = sc_o.rearrange("(m r) t -> m r t", m=NC)
    v = rows[2 * cc:2 * cc + 2, 128 + 32 * c + n:128 + 32 * c + n + 17:16, :]
    v = v.transpose([1, 0, 2])
    return v.unsqueeze(1).broadcast_to([2, 64, 2, v.shape[-1]])


def _csrc(sc_o, cc, c, n):
    rows = sc_o.rearrange("(m r) t -> m r t", m=NC)
    v = rows[2 * cc:2 * cc + 2, 256 + 128 * c + n:256 + 128 * c + n + 113:16, :]
    v = v.transpose([1, 0, 2])
    return v.unsqueeze(1).broadcast_to([8, 16, 2, v.shape[-1]])



def kernel(**inputs):
    f32 = np.float32
    x = np.ascontiguousarray(np.asarray(inputs["L0_attn_a_x"], f32)[0])
    Wq = np.asarray(inputs["L0_attn_a_Wq"], f32)
    Wk = np.asarray(inputs["L0_attn_a_Wk"], f32)
    Wv = np.asarray(inputs["L0_attn_a_Wv"], f32)
    Wo = np.asarray(inputs["L0_attn_a_Wo"], f32)
    w_in = np.asarray(inputs["L0_attn_b_in_proj_w"], f32)
    gate_w = np.asarray(inputs["L0_attn_b_gate_w"], f32)
    up_w = np.asarray(inputs["L0_attn_b_up_w"], f32)
    down_w = np.asarray(inputs["L0_attn_b_down_w"], f32)
    outp_w = np.asarray(inputs["L0_attn_b_out_proj_w"], f32)
    gup = np.asarray(inputs["L0_ffn_gate_up_packed"], np.int32)
    gus = np.asarray(inputs["L0_ffn_gate_up_scales"])
    dnp = np.asarray(inputs["L0_ffn_down_packed"], np.int32)
    dns = np.asarray(inputs["L0_ffn_down_scales"])
    dtw = np.asarray(inputs["L0_attn_b_dt_proj_w"], f32)
    dtb = np.asarray(inputs["L0_attn_b_dt_proj_b"], f32)
    cw = np.asarray(inputs["L0_attn_b_conv_w"], f32)[:, 0, :]
    cb = np.asarray(inputs["L0_attn_b_conv_b"], f32)
    alog = np.asarray(inputs["L0_attn_b_A_log"], f32)
    dpar = np.asarray(inputs["L0_attn_b_D_param"], f32)
    ln1 = np.asarray(inputs["L0_attn_b_ln1_w"], f32)
    ln2 = np.asarray(inputs["L0_attn_b_ln2_w"], f32)
    lq1 = np.asarray(inputs["L0_attn_a_lq1"], f32)
    lk1 = np.asarray(inputs["L0_attn_a_lk1"], f32)
    lq2 = np.asarray(inputs["L0_attn_a_lq2"], f32)
    lk2 = np.asarray(inputs["L0_attn_a_lk2"], f32)

    gusT = np.ascontiguousarray(gus.T)
    dnsT = np.ascontiguousarray(dns.T)
    iden = np.eye(128, dtype=f32)

    in_maps = []
    for i in range(NC):
        sl256 = slice(256 * i, 256 * (i + 1))
        sl512 = slice(512 * i, 512 * (i + 1))
        m = {
            "x0": np.ascontiguousarray(x[TL * i:TL * (i + 1)]),
            "wq_s": np.ascontiguousarray(Wq[sl256]),
            "wk_s": np.ascontiguousarray(Wk[sl256]),
            "wv_s": np.ascontiguousarray(Wv[sl256]),
            "wo_s": np.ascontiguousarray(Wo[sl256]),
            "w_in_s": np.ascontiguousarray(w_in[:, sl256]),
            "gate_s": np.ascontiguousarray(gate_w[:, sl256]),
            "up_s": np.ascontiguousarray(up_w[:, sl256]),
            "down_s": np.ascontiguousarray(down_w[:, 1024 * i:1024 * (i + 1)]),
            "outp_s": np.ascontiguousarray(outp_w[:, sl512]),
            "gup_s": np.ascontiguousarray(gup[:, 32 * i:32 * (i + 1)]),
            "gus_s": np.ascontiguousarray(gusT[:, 2 * i:2 * (i + 1)]),
            "dnp_s": np.ascontiguousarray(dnp[sl256]),
            "dns_s": np.ascontiguousarray(dnsT[sl256]),
            "dtw_s": np.ascontiguousarray(dtw[sl512]),
            "convw_s": np.ascontiguousarray(
                cw[sl512].reshape(4, 128, 4).transpose(1, 0, 2).reshape(128, 16)),
            "convb_s": np.ascontiguousarray(
                cb[sl512].reshape(4, 128).T.reshape(128, 4)),
            "alog_s": np.ascontiguousarray(
                alog[sl512].reshape(4, 128, 16).transpose(1, 0, 2).reshape(128, 64)),
            "dpar_s": np.ascontiguousarray(
                dpar[sl512].reshape(4, 128).T.reshape(128, 4)),
            "dtb_s": np.ascontiguousarray(
                dtb[sl512].reshape(4, 128).T.reshape(128, 4)),
            "ln1": ln1.reshape(1, E), "ln2": ln2.reshape(1, E),
            "lq1": lq1.reshape(1, HD), "lk1": lk1.reshape(1, HD),
            "lq2": lq2.reshape(1, HD), "lk2": lk2.reshape(1, HD),
            "iden": iden,
        }
        in_maps.append(m)

    nc_prog = build_program()
    res = run_bass_kernel_spmd(nc_prog, in_maps, core_ids=list(range(NC)))
    out = np.concatenate([np.asarray(r["out"], np.float32)
                          for r in res.results], axis=0)
    return out.reshape(1, T, E)


# version 02
# speedup vs baseline: 1.4709x; 1.4709x over previous
import numpy as np
from contextlib import ExitStack

import concourse.bass as bass
import concourse.mybir as mybir
from concourse.bass_utils import run_bass_kernel_spmd
from concourse.aot_env import allow_ici_all_to_all

F32 = mybir.dt.float32
BF16 = mybir.dt.bfloat16
F16 = mybir.dt.float16
I32 = mybir.dt.int32
AF = mybir.ActivationFunctionType
ALU = mybir.AluOpType
AX = mybir.AxisListType

NC = 8
T = 8192
TL = T // NC
E = 2048
DH = 16
HD = 64
DV = 128
DI = 4096
DXB = 1024
DST = 16
DTR = 128
INTER = 8192
IQ = 5632
PDIM = 2 * DI + 2 * DXB + DTR
DSL = DI // NC
LAM_INIT = float(0.8 - 0.6 * np.exp(-0.3 * 12))
SHIFT = 4.0

_UID = [0]
def _uname(pfx):
    _UID[0] += 1
    return f"{pfx}{_UID[0]}"

STOP_AFTER = None
DEBUG_TAPS = ()


class Prog:
    def __init__(self, nc, stack):
        self.nc = nc
        self.stack = stack
        self.sem = {}
        self.cnt = {}

    def new_sem(self, name):
        h = self.stack.enter_context(self.nc.semaphore(name))
        self.sem[name] = h
        self.cnt[name] = 0
        return h

    def inc(self, inst, name, n=1):
        inst.then_inc(self.sem[name], n)
        self.cnt[name] += n
        return self.cnt[name]

    def wait(self, eng, name, val=None):
        eng.wait_ge(self.sem[name], self.cnt[name] if val is None else val)


def build_program():
    nc = bass.Bass("TRN2", target_bir_lowering=False, debug=False)

    def inp(name, shape, dtype=F32):
        return nc.dram_tensor(name, list(shape), dtype, kind="ExternalInput").ap()

    x0_in = inp("x0", [TL, E])
    wq_in = inp("wq_s", [256, E])
    wk_in = inp("wk_s", [256, E])
    wv_in = inp("wv_s", [256, E])
    wo_in = inp("wo_s", [256, E])
    w_in_in = inp("w_in_s", [PDIM, 256])
    gate_in = inp("gate_s", [INTER, 256])
    up_in = inp("up_s", [INTER, 256])
    down_in = inp("down_s", [E, 1024])
    outp_in = inp("outp_s", [E, 512])
    gup_in = inp("gup_s", [2 * IQ, 32], I32)
    gus_in = inp("gus_s", [2 * IQ, 2], F16)
    dnp_in = inp("dnp_s", [256, IQ // 8], I32)
    dns_in = inp("dns_s", [256, 44], F16)
    dtw_in = inp("dtw_s", [DSL, DTR])
    convw_in = inp("convw_s", [128, 16])
    convb_in = inp("convb_s", [128, 4])
    alog_in = inp("alog_s", [128, 64])
    dpar_in = inp("dpar_s", [128, 4])
    dtb_in = inp("dtb_s", [128, 4])
    ln1_in = inp("ln1", [1, E])
    ln2_in = inp("ln2", [1, E])
    lq1_in = inp("lq1", [1, HD]); lk1_in = inp("lk1", [1, HD])
    lq2_in = inp("lq2", [1, HD]); lk2_in = inp("lk2", [1, HD])
    iden_in = inp("iden", [128, 128])

    out_t = nc.dram_tensor("out", [TL, E], F32, kind="ExternalOutput").ap()
    tapd = {}
    for tname in DEBUG_TAPS:
        shp = {"ln0T": [128, 16 * TL], "qkvT": [NC * 768, TL],
               "xin": [TL, E], "attO": [NC * 256, TL],
               "fh2": [128, 16 * 512], "fhm": [128, 11 * 512],
               "fya": [128, 8192], "wdnchk": [IQ, 256], "wguchk": [256, 2 * IQ],
               "wdg": [128, 22528],
               "x1": [TL, E], "x2": [TL, E], "zx": [NC * 768, TL],
               "ys": [NC * DSL, TL], "r1": [TL, E], "x3": [TL, E]}[tname]
        dt_ = BF16 if tname in ("ln0T", "qkvT", "zx", "ys", "attO", "fh2", "fhm", "wdnchk", "wguchk", "wdg") else F32
        tapd[tname] = nc.dram_tensor("tap_" + tname, shp, dt_,
                                     kind="ExternalOutput").ap()

    def idram(name, shape, dtype=BF16):
        return nc.dram_tensor(name, list(shape), dtype)

    agw = {}
    def ag_pair(name, rows_per_core, cols):
        agw[name] = (idram("agi_" + name, [rows_per_core, cols]),
                     idram("ago_" + name, [NC * rows_per_core, cols]))
    ag_pair("wq", 256, E); ag_pair("wk", 256, E)
    ag_pair("wv", 256, E); ag_pair("wo", 256, E)
    ag_pair("w_in", 256, PDIM)
    ag_pair("gate", 256, INTER); ag_pair("up", 256, INTER)
    ag_pair("down", 1024, E)
    ag_pair("outp", 512, E)
    ag_pair("wgu", 256, 2 * IQ)
    ag_pair("wdn", IQ, 256)

    a2a_qkv_i = idram("a2a_qkv_i", [NC * 768, TL])
    a2a_qkv_o = idram("a2a_qkv_o", [NC * 768, TL])
    a2a_o_i = idram("a2a_o_i", [NC * 256, TL])
    a2a_o_o = idram("a2a_o_o", [NC * 256, TL])
    a2a_sc_i = idram("a2a_sc_i", [NC * 768, TL])
    a2a_sc_o = idram("a2a_sc_o", [NC * 768, TL])
    ag_dt_i = idram("ag_dt_i", [DTR, TL])
    ag_dt_o = idram("ag_dt_o", [NC * DTR, TL])
    a2a_y_i = idram("a2a_y_i", [NC * DSL, TL])
    a2a_y_o = idram("a2a_y_o", [NC * DSL, TL])
    z_spill = idram("z_spill", [DI, TL])
    x2_spill = idram("x2_spill", [TL, E], F32)

    stack = ExitStack()
    with stack:
        p = Prog(nc, stack)
        for s in ("g", "s", "t", "v", "a", "q"):
            p.new_sem(s)
        ccsems = [p.new_sem(f"cc{i}") for i in range(20)]
        cc_n = [0]

        gp, sy, te, ve, ac = nc.gpsimd, nc.sync, nc.tensor, nc.vector, nc.scalar

        def collective(kind, ins, outs):
            sem = ccsems[cc_n[0]]; cc_n[0] += 1
            if kind == "AllToAll":
                with allow_ici_all_to_all():
                    inst = gp.collective_compute(
                        kind, ALU.bypass, replica_groups=[list(range(NC))],
                        ins=ins, outs=outs)
            else:
                inst = gp.collective_compute(
                    kind, ALU.bypass, replica_groups=[list(range(NC))],
                    ins=ins, outs=outs)
            inst.then_inc(sem)
            return sem

        sbg = lambda *a, **k: stack.enter_context(nc.sbuf_tensor(*a, **k))

        def V(inst):
            p.inc(inst, "v"); p.wait(ve, "v")
        def A(inst):
            p.inc(inst, "a"); p.wait(ac, "a")
        def Q(inst):
            p.inc(inst, "q"); p.wait(gp, "q")
        def Tm(inst):
            p.inc(inst, "t")
        def dma(eng, out, in_, name="g"):
            p.inc(eng.dma_start(out=out, in_=in_), name, 16)

        R = dict(side="right")
        iden_bf = sbg("iden_bf", [128, 128], BF16, **R)
        ones_col = sbg("ones_col", [128, 1], BF16, **R)
        ones_row = sbg("ones_row", [1, 128], BF16, **R)
        lam_sb = sbg("lam_sb", [1, 8], F32, **R)
        small = sbg("small", [1, 512], F32, **R)
        col = sbg("colstats", [128, 16], F32, **R)
        sq_scr = sbg("sq_scr", [128, E], BF16, **R)
        h_nat = sbg("h_nat", [128, E], BF16, **R)
        stg = sbg("stg", [128, 1024], BF16, **R)
        dtwT = sbg("dtwT", [128, 512], BF16, **R)
        attc = sbg("attc", [128, 2], F32, **R)

        shared = {"tr_ps": None, "x": None}

        dma(gp, iden_bf[:, :], iden_in)
        V(ve.memset(ones_col[:, :], 1.0))
        V(ve.memset(ones_row[:, :], 1.0))
        V(ve.memset(attc[:, 0:1], -SHIFT))
        V(ve.memset(attc[:, 1:2], 0.125))
        ac.wait_ge(p.sem["v"], p.cnt["v"])
        dma(gp, small[0:1, 0:HD], lq1_in)
        dma(gp, small[0:1, HD:2 * HD], lk1_in)
        dma(gp, small[0:1, 2 * HD:3 * HD], lq2_in)
        dma(gp, small[0:1, 3 * HD:4 * HD], lk2_in)
        p.wait(ve, "g")
        V(ve.tensor_mul(small[0:1, 256:256 + HD], small[0:1, 0:HD],
                        small[0:1, HD:2 * HD]))
        V(ve.tensor_mul(small[0:1, 320:320 + HD], small[0:1, 2 * HD:3 * HD],
                        small[0:1, 3 * HD:4 * HD]))
        V(ve.reduce_sum(small[0:1, 384:385], small[0:1, 256:256 + HD], axis=AX.X))
        V(ve.reduce_sum(small[0:1, 385:386], small[0:1, 320:320 + HD], axis=AX.X))
        p.wait(ac, "v")
        A(ac.activation(small[0:1, 388:390], small[0:1, 384:386], AF.Exp))
        p.wait(ve, "a")
        V(ve.tensor_sub(small[0:1, 390:391], small[0:1, 388:389],
                        small[0:1, 389:390]))
        V(ve.tensor_scalar_add(lam_sb[0:1, 0:1], small[0:1, 390:391], LAM_INIT))

        def emit_rstd(src_view, mean_sub, eps):
            p.wait(ac, "v")
            A(ac.activation(sq_scr[:, :], src_view, AF.Square,
                            accum_out=col[:, 0:1]))
            if mean_sub:
                V(ve.reduce_sum(col[:, 1:2], src_view, axis=AX.X))
                p.wait(ve, "a")
                V(ve.tensor_scalar_mul(col[:, 2:3], col[:, 1:2], 1.0 / E))
                V(ve.tensor_scalar_mul(col[:, 3:4], col[:, 0:1], 1.0 / E))
                V(ve.tensor_mul(col[:, 4:5], col[:, 2:3], col[:, 2:3]))
                V(ve.tensor_tensor(col[:, 5:6], col[:, 3:4], col[:, 4:5],
                                   op=ALU.subtract))
            else:
                p.wait(ve, "a")
                V(ve.tensor_scalar_mul(col[:, 5:6], col[:, 0:1], 1.0 / E))
            V(ve.tensor_scalar_add(col[:, 6:7], col[:, 5:6], eps))
            p.wait(ac, "v")
            A(ac.sqrt(col[:, 7:8], col[:, 6:7]))
            p.wait(ve, "a")
            V(ve.reciprocal(col[:, 8:9], col[:, 7:8]))

        def emit_ln(src_view, dst_bf, mean_sub=True, eps=1e-5):
            emit_rstd(src_view, mean_sub, eps)
            if mean_sub:
                V(ve.tensor_scalar(dst_bf, src_view, col[:, 2:3], col[:, 8:9],
                                   op0=ALU.subtract, op1=ALU.mult))
            else:
                V(ve.tensor_scalar_mul(dst_bf, src_view, col[:, 8:9]))

        def nat_to_T(src_bf, dst_col_fn):
            for half in range(2):
                pb = shared["tr_ps"][half]
                p.wait(te, "v")
                p.wait(te, "a")
                for c8 in range(8):
                    c = half * 8 + c8
                    inst = te.transpose(pb[:, c8 * 128:(c8 + 1) * 128],
                                        src_bf[:, c * 128:(c + 1) * 128],
                                        iden_bf[:, :])
                    p.inc(inst, "t")
                p.wait(ac, "t")
                for c8 in range(8):
                    c = half * 8 + c8
                    inst = ac.copy(dst_col_fn(c), pb[:, c8 * 128:(c8 + 1) * 128])
                    p.inc(inst, "a")
                p.wait(ac, "a")

        def emit_ln_bc(row_in, dst_bc, bc_ps):
            dma(gp, sq_scr[0:1, 0:E], row_in)
            p.wait(te, "g")
            for cq in range(4):
                p.wait(te, "a")
                inst = te.matmul(bc_ps[:, :], ones_row[0:1, :],
                                 sq_scr[0:1, cq * 512:(cq + 1) * 512],
                                 start=True, stop=True)
                p.inc(inst, "t")
                p.wait(ac, "t")
                A(ac.copy(dst_bc[:, cq * 512:(cq + 1) * 512], bc_ps[:, :]))

        def fence():
            for eng in (gp, sy, te, ve, ac):
                for s in ("t", "v", "a", "q", "g", "s"):
                    eng.wait_ge(p.sem[s], p.cnt[s])

        def final_output_and_done():
            fence()
            x_sb = shared["x"]
            for j in range(8):
                dma(sy, out_t[j * 128:(j + 1) * 128, :],
                    x_sb[:, j * E:(j + 1) * E], "s")
            p.wait(sy, "s")
            p.wait(gp, "s")

        ag_sems = {}
        with ExitStack() as ph:
            en = ph.enter_context
            pt_src = en(nc.sbuf_tensor("pt_src", [128, 1024], BF16))
            pt_stage = en(nc.sbuf_tensor("pt_stage", [128, 1024], BF16))
            dq_pk = en(nc.sbuf_tensor("dq_pk", [128, 704], I32))
            dq_nib = en(nc.sbuf_tensor("dq_nib", [128, 1024], I32))
            dq_tmp = en(nc.sbuf_tensor("dq_tmp", [128, 1024], I32))
            dq_msk = en(nc.sbuf_tensor("dq_msk", [128, 1024], I32))
            dq_fix = en(nc.sbuf_tensor("dq_fix", [128, 1024], F32))
            dq_s16 = en(nc.sbuf_tensor("dq_s16", [128, 44], F16))
            dq_sc = en(nc.sbuf_tensor("dq_sc", [128, 44], F32))
            dq_out = en(nc.sbuf_tensor("dq_out", [128, 1024], BF16))
            pt_ps = [en(nc.psum_tensor(_uname("ps"), [128, 1024], BF16)) for _i in range(2)]

            for name, src in (("wq", wq_in), ("wk", wk_in), ("wv", wv_in),
                              ("wo", wo_in)):
                dma(gp, agw[name][0].ap(), src)
            p.wait(gp, "g")
            for name in ("wq", "wk", "wv", "wo"):
                ag_sems[name] = collective("AllGather", [agw[name][0].ap().opt()],
                                           [agw[name][1].ap().opt()])

            def prep_transpose(src_ap, rows, cols, dst_ap, dst_row_off=0):
                nrt = rows // 128
                nct = cols // 128
                nb = max(1, 1024 // cols)
                for r0 in range(0, nrt, nb):
                    nblk = min(nb, nrt - r0)
                    p.wait(gp, "t")
                    dma(gp, pt_src.ap()[:, 0:nblk * cols].rearrange(
                            "p (b c) -> p b c", b=nblk),
                        src_ap[r0 * 128:(r0 + nblk) * 128, :].rearrange(
                            "(b p) c -> p b c", p=128))
                    p.wait(te, "g")
                    for c in range(nct):
                        pb = pt_ps[c % 2]
                        p.wait(te, "a")
                        for b in range(nblk):
                            inst = te.transpose(
                                pb[:, b * 128:(b + 1) * 128],
                                pt_src[:, b * cols + c * 128:
                                       b * cols + (c + 1) * 128],
                                iden_bf[:, :])
                            p.inc(inst, "t")
                        p.wait(ac, "t")
                        p.wait(ac, "g")
                        A(ac.copy(pt_stage[:, 0:nblk * 128],
                                  pb[:, 0:nblk * 128]))
                        p.wait(gp, "a")
                        dma(gp, dst_ap[dst_row_off + c * 128:
                                       dst_row_off + (c + 1) * 128,
                                       r0 * 128:(r0 + nblk) * 128],
                            pt_stage[:, 0:nblk * 128])
                p.wait(gp, "g")

            prep_transpose(w_in_in, PDIM - 128, 256, agw["w_in"][0].ap())
            p.wait(gp, "t")
            dma(gp, pt_src[:, 0:256], w_in_in[PDIM - 128:PDIM, :])
            p.wait(te, "g")
            p.wait(te, "a")
            for c in range(2):
                inst = te.transpose(pt_ps[0][:, c * 128:(c + 1) * 128],
                                    pt_src[:, c * 128:(c + 1) * 128],
                                    iden_bf[:, :])
                p.inc(inst, "t")
            p.wait(ac, "t")
            p.wait(ac, "g")
            A(ac.copy(pt_stage[:, 0:256], pt_ps[0][:, 0:256]))
            p.wait(gp, "a")
            for c in range(2):
                dma(gp, agw["w_in"][0].ap()[c * 128:(c + 1) * 128,
                                            PDIM - 128:PDIM],
                    pt_stage[:, c * 128:(c + 1) * 128])
            p.wait(gp, "g")
            ag_sems["w_in"] = collective("AllGather", [agw["w_in"][0].ap().opt()],
                                         [agw["w_in"][1].ap().opt()])
            prep_transpose(outp_in, E, 512, agw["outp"][0].ap())
            ag_sems["outp"] = collective("AllGather", [agw["outp"][0].ap().opt()],
                                         [agw["outp"][1].ap().opt()])
            prep_transpose(gate_in, INTER, 256, agw["gate"][0].ap())
            ag_sems["gate"] = collective("AllGather", [agw["gate"][0].ap().opt()],
                                         [agw["gate"][1].ap().opt()])
            prep_transpose(up_in, INTER, 256, agw["up"][0].ap())
            ag_sems["up"] = collective("AllGather", [agw["up"][0].ap().opt()],
                                       [agw["up"][1].ap().opt()])
            for cpass in range(4):
                prep_transpose(down_in[:, cpass * 256:(cpass + 1) * 256], E, 256,
                               agw["down"][0].ap(), dst_row_off=cpass * 256)
            ag_sems["down"] = collective("AllGather", [agw["down"][0].ap().opt()],
                                         [agw["down"][1].ap().opt()])

            def dequant_tile(words_view, incols, sc_off):
                for j in range(8):
                    inst = ve.tensor_scalar(dq_nib.ap()[:, j:incols:8],
                                            words_view, 4 * j, 15,
                                            op0=ALU.logical_shift_right,
                                            op1=ALU.bitwise_and)
                    p.inc(inst, "v")
                p.wait(ve, "v")
                V(ve.tensor_scalar(dq_tmp[:, 0:incols], dq_nib[:, 0:incols],
                                   7, None, op0=ALU.bitwise_and))
                V(ve.tensor_scalar(dq_msk[:, 0:incols], dq_tmp[:, 0:incols],
                                   2, None, op0=ALU.is_lt))
                V(ve.tensor_scalar(dq_tmp[:, 0:incols], dq_tmp[:, 0:incols],
                                   252, None, op0=ALU.add))
                V(ve.tensor_scalar(dq_tmp[:, 0:incols], dq_tmp[:, 0:incols],
                                   22, None, op0=ALU.logical_shift_left))
                V(ve.tensor_scalar(dq_fix[:, 0:incols],
                                   dq_tmp.ap().bitcast(F32)[:, 0:incols],
                                   2.0, 1.0, op0=ALU.mult, op1=ALU.subtract))
                V(ve.select(dq_tmp.ap().bitcast(F32)[:, 0:incols],
                            dq_msk[:, 0:incols], dq_fix[:, 0:incols],
                            dq_tmp.ap().bitcast(F32)[:, 0:incols]))
                V(ve.tensor_scalar(dq_nib[:, 0:incols], dq_nib[:, 0:incols],
                                   8, 28, op0=ALU.bitwise_and,
                                   op1=ALU.logical_shift_left))
                V(ve.tensor_tensor(dq_tmp[:, 0:incols], dq_tmp[:, 0:incols],
                                   dq_nib[:, 0:incols], op=ALU.bitwise_or))
                p.wait(ve, "t")
                for gq in range(incols // 128):
                    inst = ve.tensor_scalar_mul(
                        dq_out[:, gq * 128:(gq + 1) * 128],
                        dq_tmp.ap().bitcast(F32)[:, gq * 128:(gq + 1) * 128],
                        dq_sc[:, sc_off + gq:sc_off + gq + 1])
                    p.inc(inst, "v")
                p.wait(ve, "v")

            for r in range(2 * IQ // 128):
                p.wait(gp, "v")
                p.wait(gp, "a")
                dma(gp, dq_pk[:, 0:32], gup_in[r * 128:(r + 1) * 128, :])
                dma(gp, dq_s16[:, 0:2], gus_in[r * 128:(r + 1) * 128, :])
                p.wait(ve, "g")
                V(ve.tensor_copy(dq_sc[:, 0:2], dq_s16[:, 0:2]))
                dequant_tile(dq_pk[:, 0:32], 256, 0)
                pb = pt_ps[r % 2]
                p.wait(te, "v")
                p.wait(te, "a")
                for c in range(2):
                    inst = te.transpose(pb[:, c * 128:(c + 1) * 128],
                                        dq_out[:, c * 128:(c + 1) * 128],
                                        iden_bf[:, :])
                    p.inc(inst, "t")
                p.wait(ac, "t")
                p.wait(ac, "g")
                A(ac.copy(pt_stage[:, 0:256], pb[:, 0:256]))
                p.wait(gp, "a")
                for c in range(2):
                    dma(gp, agw["wgu"][0].ap()[c * 128:(c + 1) * 128,
                                               r * 128:(r + 1) * 128],
                        pt_stage[:, c * 128:(c + 1) * 128])
            p.wait(gp, "g")
            ag_sems["wgu"] = collective("AllGather", [agw["wgu"][0].ap().opt()],
                                        [agw["wgu"][1].ap().opt()])

            for r in range(2):
                p.wait(gp, "v")
                p.wait(gp, "a")
                dma(gp, dq_pk[:, 0:704], dnp_in[r * 128:(r + 1) * 128, :])
                dma(gp, dq_s16[:, 0:44], dns_in[r * 128:(r + 1) * 128, :])
                p.wait(ve, "g")
                V(ve.tensor_copy(dq_sc[:, 0:44], dq_s16[:, 0:44]))
                for seg in range(6):
                    incols = 1024 if seg < 5 else 512
                    nblk = incols // 128
                    dequant_tile(dq_pk[:, seg * 128:seg * 128 + incols // 8],
                                 incols, seg * 8)
                    pb = pt_ps[seg % 2]
                    p.wait(te, "v")
                    p.wait(te, "a")
                    for c in range(nblk):
                        inst = te.transpose(pb[:, c * 128:(c + 1) * 128],
                                            dq_out[:, c * 128:(c + 1) * 128],
                                            iden_bf[:, :])
                        p.inc(inst, "t")
                    p.wait(ac, "t")
                    p.wait(ac, "g")
                    A(ac.copy(pt_stage[:, 0:incols], pb[:, 0:incols]))
                    p.wait(gp, "a")
                    for c in range(nblk):
                        dma(gp, agw["wdn"][0].ap()[seg * 1024 + c * 128:
                                                   seg * 1024 + (c + 1) * 128,
                                                   r * 128:(r + 1) * 128],
                            pt_stage[:, c * 128:(c + 1) * 128])
            p.wait(gp, "g")
            ag_sems["wdn"] = collective("AllGather", [agw["wdn"][0].ap().opt()],
                                        [agw["wdn"][1].ap().opt()])

            for c in range(4):
                p.wait(gp, "t")
                dma(gp, pt_src[:, 0:128], dtw_in[c * 128:(c + 1) * 128, :])
                p.wait(te, "g")
                p.wait(te, "a")
                Tm(te.transpose(pt_ps[0][:, 0:128], pt_src[:, 0:128],
                                iden_bf[:, :]))
                p.wait(ac, "t")
                A(ac.copy(dtwT[:, c * 128:(c + 1) * 128], pt_ps[0][:, 0:128]))
            p.wait(gp, "a")

        if "wdnchk" in tapd:
            gp.wait_ge(ag_sems["wdn"], 1)
            dma(gp, tapd["wdnchk"], agw["wdn"][1].ap()[0:IQ, :])
            p.wait(gp, "g")
        if "wguchk" in tapd:
            gp.wait_ge(ag_sems["wgu"], 1)
            dma(gp, tapd["wguchk"], agw["wgu"][1].ap()[0:256, :])
            p.wait(gp, "g")
        fence()
        if STOP_AFTER == "prep":
            with ExitStack() as ph:
                x_sb = ph.enter_context(nc.sbuf_tensor("x_sb", [128, 8 * E], F32))
                shared["x"] = x_sb
                for sem in ag_sems.values():
                    gp.wait_ge(sem, 1)
                V(ve.memset(x_sb[:, :], 0.0))
                p.wait(sy, "v")
                final_output_and_done()
            return nc

        with ExitStack() as phx:
            x_sb = phx.enter_context(nc.sbuf_tensor("x_sb", [128, 8 * E], F32))
            shared["x"] = x_sb
            for j in range(8):
                dma(sy, x_sb[:, j * E:(j + 1) * E],
                    x0_in[j * 128:(j + 1) * 128, :], "s")

            if "xin" in tapd:
                p.wait(gp, "v"); p.wait(gp, "a"); p.wait(gp, "t")
                p.wait(gp, "s")
                for j in range(8):
                    dma(gp, tapd["xin"][j * 128:(j + 1) * 128, :],
                        x_sb[:, j * E:(j + 1) * E])
                p.wait(gp, "g")
            with ExitStack() as ph:
                en = ph.enter_context
                hT = en(nc.sbuf_tensor("hT", [128, 16 * TL], BF16))
                wfull = en(nc.sbuf_tensor("wfull", [128, 16 * E], BF16))
                tr_psl = [en(nc.psum_tensor(_uname("ps"), [128, 1024], BF16)) for _i in range(2)]
                qk_ps = [en(nc.psum_tensor(_uname("ps"), [128, 512], F32)) for _i in range(2)]
                shared["tr_ps"] = tr_psl

                p.wait(ve, "s")
                p.wait(ac, "s")
                for j in range(8):
                    p.wait(ve, "t")
                    emit_ln(x_sb[:, j * E:(j + 1) * E], h_nat[:, :], True, 1e-5)
                    nat_to_T(h_nat[:, :],
                             lambda c, j=j: hT[:, c * TL + j * 128:
                                               c * TL + (j + 1) * 128])
                if "ln0T" in tapd:
                    p.wait(gp, "a"); p.wait(gp, "v"); p.wait(gp, "t")
                    dma(gp, tapd["ln0T"], hT[:, :])
                    p.wait(gp, "g")

                def load_wfull(name):
                    p.wait(sy, "t")
                    dma(sy, wfull.ap().rearrange("p (c n) -> p c n", c=16),
                        agw[name][1].ap().rearrange("(c p) n -> p c n", p=128),
                        "s")

                def proj_qk(qk_idx):
                    for dq in range(16):
                        p.wait(ac, "g")
                        for tc in range(2):
                            pb = qk_ps[tc]
                            p.wait(te, "s")
                            p.wait(te, "a")
                            for c in range(16):
                                inst = te.matmul(
                                    pb[:, :],
                                    wfull[:, c * E + dq * 128:
                                          c * E + (dq + 1) * 128],
                                    hT[:, c * TL + tc * 512:
                                       c * TL + (tc + 1) * 512],
                                    start=(c == 0), stop=(c == 15))
                            p.inc(inst, "t")
                            p.wait(ac, "t")
                            A(ac.copy(stg[:, tc * 512:(tc + 1) * 512], pb[:, :]))
                        p.wait(gp, "a")
                        j = dq // 2
                        roff = qk_idx * 256 + (dq % 2) * 128
                        dma(gp, a2a_qkv_i.ap()[j * 768 + roff:
                                               j * 768 + roff + 128, :],
                            stg[:, 0:TL])

                sy.wait_ge(ag_sems["wq"], 1)
                load_wfull("wq")
                proj_qk(0)
                sy.wait_ge(ag_sems["wk"], 1)
                load_wfull("wk")
                proj_qk(1)
                sy.wait_ge(ag_sems["wv"], 1)
                load_wfull("wv")
                a2ai_flat = a2a_qkv_i.ap().rearrange("(m r) t -> m (r t)", m=NC)
                for j in range(8):
                    for ec in range(4):
                        pb = qk_ps[ec % 2]
                        p.wait(te, "s")
                        p.wait(te, "a")
                        for c in range(16):
                            inst = te.matmul(
                                pb[:, :],
                                hT[:, c * TL + j * 128:c * TL + (j + 1) * 128],
                                wfull[:, c * E + ec * 512:c * E + (ec + 1) * 512],
                                start=(c == 0), stop=(c == 15))
                        p.inc(inst, "t")
                        p.wait(ac, "t")
                        p.wait(ac, "g")
                        A(ac.copy(stg[:, 0:512], pb[:, :]))
                        p.wait(gp, "a")
                        for half in range(2):
                            jj = ec * 2 + half
                            base = 512 * 1024 + 32768 * j
                            dstv = a2ai_flat[jj:jj + 1, base:base + 32768] \
                                .rearrange("one (pp c) -> (one pp) c", c=256)
                            dma(gp, dstv, stg[:, half * 256:(half + 1) * 256])
                p.wait(gp, "g")
                a2a_qkv_sem = collective("AllToAll", [a2a_qkv_i.ap().opt()],
                                         [a2a_qkv_o.ap().opt()])
                if "qkvT" in tapd:
                    gp.wait_ge(a2a_qkv_sem, 1)
                    dma(gp, tapd["qkvT"], a2a_qkv_o.ap())
                    p.wait(gp, "g")

            fence()
            with ExitStack() as ph:
                en = ph.enter_context
                qt_sb = en(nc.sbuf_tensor("qt_sb", [128, 2 * T], BF16))
                kt_sb = en(nc.sbuf_tensor("kt_sb", [128, 2 * T], BF16))
                v_sb = en(nc.sbuf_tensor("v_sb", [128, 64 * 256], BF16))
                e_sb = en(nc.sbuf_tensor("e_sb", [128, 2048], BF16))
                att_f = en(nc.sbuf_tensor("att_f", [1, 2048], F32))
                att_b = en(nc.sbuf_tensor("att_b", [1, 1024], BF16))
                abc = en(nc.sbuf_tensor("abc", [128, 1024], F32))
                od_sb = en(nc.sbuf_tensor("od_sb", [128, 512], F32))
                t0_sb = en(nc.sbuf_tensor("t0_sb", [128, 512], F32))
                sqv = en(nc.sbuf_tensor("sqv", [128, 512], BF16))
                ofin = en(nc.sbuf_tensor("ofin", [128, 512], BF16))
                sc_ps = [en(nc.psum_tensor(_uname("ps"), [128, 512], F32)) for _i in range(4)]
                o_ps = [en(nc.psum_tensor(_uname("ps"), [128, 512], F32)) for _i in range(2)]
                s_ps = [en(nc.psum_tensor(_uname("ps"), [1, 512], F32)) for _i in range(2)]

                sy.wait_ge(a2a_qkv_sem, 1)
                qkvo_r = a2a_qkv_o.ap().rearrange("(m b pp) t -> pp b m t",
                                                  m=NC, b=6, pp=128)
                for b in range(2):
                    dma(sy, qt_sb.ap()[:, b * T:(b + 1) * T].rearrange(
                            "p (m t) -> p m t", m=8),
                        qkvo_r[:, b, :, :], "s")
                    dma(sy, kt_sb.ap()[:, b * T:(b + 1) * T].rearrange(
                            "p (m t) -> p m t", m=8),
                        qkvo_r[:, 2 + b, :, :], "s")
                a2ao_flat = a2a_qkv_o.ap().rearrange("(m r) t -> m (r t)", m=NC)
                vsrc = a2ao_flat[:, 512 * 1024:768 * 1024].rearrange(
                    "m (t8 pp c) -> pp m t8 c", t8=8, pp=128, c=256)
                for m in range(8):
                    dma(sy, v_sb.ap().rearrange("p (m t8 c) -> p m t8 c",
                                                m=8, t8=8)[:, m, :, :],
                        vsrc[:, m, :, :], "s")
                p.wait(te, "s")

                exp_done = {}
                attnv_done = {}
                sc_done = {}

                def emit_sc(dh, qt, kt):
                    bi = kt % 2
                    if (dh, qt, kt - 2) in exp_done:
                        te.wait_ge(p.sem["a"], exp_done[(dh, qt, kt - 2)])
                    else:
                        te.wait_ge(p.sem["a"], p.cnt["a"])
                        te.wait_ge(p.sem["v"], p.cnt["v"])
                    te.matmul(sc_ps[2 * bi][:, :],
                              kt_sb[0:64, dh * T + kt * 128:
                                    dh * T + (kt + 1) * 128],
                              qt_sb[0:64, dh * T + qt * 512:
                                    dh * T + (qt + 1) * 512],
                              start=True, stop=True, tile_position=(0, 0))
                    inst = te.matmul(sc_ps[2 * bi + 1][:, :],
                                     kt_sb[64:128, dh * T + kt * 128:
                                           dh * T + (kt + 1) * 128],
                                     qt_sb[64:128, dh * T + qt * 512:
                                           dh * T + (qt + 1) * 512],
                                     start=True, stop=True,
                                     tile_position=(64, 0))
                    p.inc(inst, "t")
                    sc_done[(dh, qt, kt)] = p.cnt["t"]

                for dh in range(2):
                    for qt in range(16):
                        nkt = 4 * qt + 4
                        emit_sc(dh, qt, 0)
                        for kt in range(nkt):
                            bi = kt % 2
                            if kt + 1 < nkt:
                                emit_sc(dh, qt, kt + 1)
                            ac.wait_ge(p.sem["t"], sc_done[(dh, qt, kt)])
                            if (dh, qt, kt - 2) in attnv_done:
                                ac.wait_ge(p.sem["t"],
                                           attnv_done[(dh, qt, kt - 2)])
                            ac.activation(e_sb[:, (2 * bi) * 512:
                                               (2 * bi + 1) * 512],
                                          sc_ps[2 * bi][:, :], AF.Exp,
                                          bias=attc[:, 0:1], scale=attc[:, 1:2])
                            inst = ac.activation(
                                e_sb[:, (2 * bi + 1) * 512:(2 * bi + 2) * 512],
                                sc_ps[2 * bi + 1][:, :], AF.Exp,
                                bias=attc[:, 0:1], scale=attc[:, 1:2])
                            p.inc(inst, "a", 2)
                            exp_done[(dh, qt, kt)] = p.cnt["a"]
                            diag = kt >= 4 * qt
                            if diag:
                                gp.wait_ge(p.sem["a"], exp_done[(dh, qt, kt)])
                                gp.affine_select(
                                    e_sb[:, (2 * bi) * 512:(2 * bi + 1) * 512],
                                    e_sb[:, (2 * bi) * 512:(2 * bi + 1) * 512],
                                    [[1, 512]], compare_op=ALU.is_ge, fill=0.0,
                                    base=512 * qt - 128 * kt,
                                    channel_multiplier=-1)
                                inst = gp.affine_select(
                                    e_sb[:, (2 * bi + 1) * 512:
                                         (2 * bi + 2) * 512],
                                    e_sb[:, (2 * bi + 1) * 512:
                                         (2 * bi + 2) * 512],
                                    [[1, 512]], compare_op=ALU.is_ge, fill=0.0,
                                    base=512 * qt - 128 * kt,
                                    channel_multiplier=-1)
                                p.inc(inst, "q", 2)
                                te.wait_ge(p.sem["q"], p.cnt["q"])
                            else:
                                te.wait_ge(p.sem["a"], exp_done[(dh, qt, kt)])
                            e0 = e_sb[:, (2 * bi) * 512:(2 * bi + 1) * 512]
                            e1 = e_sb[:, (2 * bi + 1) * 512:(2 * bi + 2) * 512]
                            vt = v_sb[:, kt * 256 + dh * 128:
                                      kt * 256 + (dh + 1) * 128]
                            st = (kt == 0)
                            sp = (kt == nkt - 1)
                            if st:
                                te.wait_ge(p.sem["v"], p.cnt["v"])
                            te.matmul(o_ps[0][:, :], vt, e0, start=st, stop=sp)
                            te.matmul(o_ps[1][:, :], vt, e1, start=st, stop=sp)
                            te.matmul(s_ps[0][0:1, :], ones_col[:, :], e0,
                                      start=st, stop=sp)
                            inst = te.matmul(s_ps[1][0:1, :], ones_col[:, :], e1,
                                             start=st, stop=sp)
                            p.inc(inst, "t")
                            attnv_done[(dh, qt, kt)] = p.cnt["t"]

                        p.wait(ve, "t")
                        V(ve.reciprocal(att_f[0:1, 0:512], s_ps[0][0:1, :]))
                        V(ve.reciprocal(att_f[0:1, 512:1024], s_ps[1][0:1, :]))
                        V(ve.tensor_scalar_mul(att_f[0:1, 512:1024],
                                               att_f[0:1, 512:1024],
                                               lam_sb[0:1, 0:1]))
                        V(ve.tensor_copy(att_b[0:1, 0:512], att_f[0:1, 0:512]))
                        V(ve.tensor_copy(att_b[0:1, 512:1024],
                                         att_f[0:1, 512:1024]))
                        p.wait(te, "v")
                        te.matmul(sc_ps[0][:, :], ones_row[0:1, :],
                                  att_b[0:1, 0:512], start=True, stop=True)
                        inst = te.matmul(sc_ps[1][:, :], ones_row[0:1, :],
                                         att_b[0:1, 512:1024],
                                         start=True, stop=True)
                        p.inc(inst, "t")
                        p.wait(ac, "t")
                        ac.copy(abc[:, 0:512], sc_ps[0][:, :])
                        A(ac.copy(abc[:, 512:1024], sc_ps[1][:, :]))
                        p.wait(ve, "a")
                        V(ve.tensor_tensor(t0_sb[:, :], o_ps[0][:, :],
                                           abc[:, 0:512], op=ALU.mult))
                        V(ve.tensor_tensor(od_sb[:, :], o_ps[1][:, :],
                                           abc[:, 512:1024], op=ALU.mult))
                        V(ve.tensor_tensor(od_sb[:, :], t0_sb[:, :],
                                           od_sb[:, :], op=ALU.subtract))
                        p.wait(ac, "v")
                        A(ac.activation(sqv[:, :], od_sb[:, :], AF.Square))
                        p.wait(te, "a")
                        inst = te.matmul(s_ps[0][0:1, :], ones_col[:, :],
                                         sqv[:, :], start=True, stop=True)
                        p.inc(inst, "t")
                        p.wait(ve, "t")
                        V(ve.tensor_scalar(att_f[0:1, 1024:1536],
                                           s_ps[0][0:1, :], 1.0 / 128, 1e-5,
                                           op0=ALU.mult, op1=ALU.add))
                        p.wait(ac, "v")
                        A(ac.sqrt(att_f[0:1, 1536:2048], att_f[0:1, 1024:1536]))
                        p.wait(ve, "a")
                        V(ve.reciprocal(att_f[0:1, 1024:1536],
                                        att_f[0:1, 1536:2048]))
                        V(ve.tensor_scalar_mul(att_b[0:1, 0:512],
                                               att_f[0:1, 1024:1536],
                                               1.0 - LAM_INIT))
                        p.wait(te, "v")
                        inst = te.matmul(sc_ps[2][:, :], ones_row[0:1, :],
                                         att_b[0:1, 0:512],
                                         start=True, stop=True)
                        p.inc(inst, "t")
                        p.wait(ac, "t")
                        A(ac.copy(abc[:, 0:512], sc_ps[2][:, :]))
                        p.wait(ve, "a")
                        V(ve.tensor_tensor(ofin[:, :], od_sb[:, :],
                                           abc[:, 0:512], op=ALU.mult))
                        p.wait(gp, "v")
                        j = qt // 2
                        dma(gp, a2a_o_i.ap()[j * 256 + dh * 128:
                                             j * 256 + (dh + 1) * 128,
                                             (qt % 2) * 512:(qt % 2) * 512 + 512],
                            ofin[:, :])
                p.wait(gp, "g")
                a2a_o_sem = collective("AllToAll", [a2a_o_i.ap().opt()],
                                       [a2a_o_o.ap().opt()])

            fence()
            with ExitStack() as ph:
                en = ph.enter_context
                otT = en(nc.sbuf_tensor("otT", [128, 16 * TL], BF16))
                wo_sb = en(nc.sbuf_tensor("wo_sb", [128, 16 * E], BF16))
                wo_ps = [en(nc.psum_tensor(_uname("ps"), [128, 512], F32)) for _i in range(2)]
                if "attO" in tapd:
                    gp.wait_ge(a2a_o_sem, 1)
                    dma(gp, tapd["attO"], a2a_o_o.ap())
                    p.wait(gp, "g")
                sy.wait_ge(a2a_o_sem, 1)
                dma(sy, otT.ap().rearrange("p (c t) -> p c t", c=16),
                    a2a_o_o.ap().rearrange("(c p) t -> p c t", p=128), "s")
                sy.wait_ge(ag_sems["wo"], 1)
                dma(sy, wo_sb.ap().rearrange("p (c n) -> p c n", c=16),
                    agw["wo"][1].ap().rearrange("(c p) n -> p c n", p=128), "s")
                p.wait(te, "s")
                for tt in range(8):
                    for ec in range(4):
                        pb = wo_ps[ec % 2]
                        p.wait(te, "v")
                        for c in range(16):
                            inst = te.matmul(
                                pb[:, :],
                                otT[:, c * TL + tt * 128:c * TL + (tt + 1) * 128],
                                wo_sb[:, c * E + ec * 512:c * E + (ec + 1) * 512],
                                start=(c == 0), stop=(c == 15))
                        p.inc(inst, "t")
                        p.wait(ve, "t")
                        V(ve.tensor_tensor(x_sb[:, tt * E + ec * 512:
                                                tt * E + (ec + 1) * 512],
                                           x_sb[:, tt * E + ec * 512:
                                                tt * E + (ec + 1) * 512],
                                           pb[:, :], op=ALU.add))
                if "x1" in tapd:
                    p.wait(gp, "v"); p.wait(gp, "a"); p.wait(gp, "t")
                    for j in range(8):
                        dma(gp, tapd["x1"][j * 128:(j + 1) * 128, :],
                            x_sb[:, j * E:(j + 1) * E])
                    p.wait(gp, "g")

            fence()
            if STOP_AFTER == "attn":
                final_output_and_done()
                return nc

            def emit_qffn(tag):
                with ExitStack() as ph:
                    en = ph.enter_context
                    h2T = en(nc.sbuf_tensor(f"h2T{tag}", [128, 16 * 512], BF16))
                    hTm = en(nc.sbuf_tensor(f"hTm{tag}", [128, 11 * 512], BF16))
                    wdng = en(nc.sbuf_tensor(f"wdng{tag}", [128, 22528], BF16))
                    y_acc = en(nc.sbuf_tensor(f"y_acc{tag}", [128, 8192], F32))
                    wgu_b = [en(nc.sbuf_tensor(f"wgu{tag}{i}", [128, 4096], BF16))
                             for i in range(2)]
                    sg_st = en(nc.sbuf_tensor(f"sg_st{tag}", [128, 512], BF16))
                    tr_psf = [en(nc.psum_tensor(_uname("ps"), [128, 1024], BF16))
                              for _ in range(2)]
                    gu_ps = [en(nc.psum_tensor(_uname("ps"), [128, 512], F32))
                             for _ in range(4)]
                    dn_ps = [en(nc.psum_tensor(_uname("ps"), [128, 256], F32))
                             for _ in range(2)]
                    shared["tr_ps"] = tr_psf
                    gp.wait_ge(ag_sems["wgu"], 1)
                    gp.wait_ge(ag_sems["wdn"], 1)
                    sy.wait_ge(ag_sems["wgu"], 1)
                    sy.wait_ge(ag_sems["wdn"], 1)
                    for hs in range(2):
                        jt = [4 * hs + k for k in range(4)]
                        for k, j in enumerate(jt):
                            p.wait(ve, "t")
                            emit_ln(x_sb[:, j * E:(j + 1) * E], h_nat[:, :],
                                    True, 1e-5)
                            nat_to_T(h_nat[:, :],
                                     lambda c, k=k: h2T[:, c * 512 + k * 128:
                                                        c * 512 + (k + 1) * 128])
                        if hs == 0 and tag == "a" and "fh2" in tapd:
                            p.wait(gp, "a"); p.wait(gp, "v")
                            dma(gp, tapd["fh2"], h2T[:, :])
                            p.wait(gp, "g")
                        mm_done = {}
                        evac_done = {}
                        for grp in range(4):
                            for ipx in range(11):
                                ic = grp * 11 + ipx
                                wb = wgu_b[ic % 2]
                                if ic >= 2:
                                    sy.wait_ge(p.sem["t"], mm_done[ic - 2])
                                else:
                                    p.wait(sy, "t")
                                dma(sy, wb[:, 0:2048].rearrange(
                                        "p (c w) -> p c w", c=16),
                                    agw["wgu"][1].ap().rearrange(
                                        "(c p) n -> p c n", p=128)
                                    [:, :, ic * 128:(ic + 1) * 128], "s")
                                dma(sy, wb[:, 2048:4096].rearrange(
                                        "p (c w) -> p c w", c=16),
                                    agw["wgu"][1].ap().rearrange(
                                        "(c p) n -> p c n", p=128)
                                    [:, :, IQ + ic * 128:IQ + (ic + 1) * 128],
                                    "s")
                                te.wait_ge(p.sem["s"], p.cnt["s"])
                                if ic >= 2:
                                    te.wait_ge(p.sem["a"], evac_done[ic - 2][0])
                                    te.wait_ge(p.sem["v"], evac_done[ic - 2][1])
                                else:
                                    p.wait(te, "a")
                                    p.wait(te, "v")
                                gA = gu_ps[2 * (ic % 2)]
                                gB = gu_ps[2 * (ic % 2) + 1]
                                for c in range(16):
                                    inst = te.matmul(
                                        gA[:, :],
                                        wb[:, c * 128:(c + 1) * 128],
                                        h2T[:, c * 512:(c + 1) * 512],
                                        start=(c == 0), stop=(c == 15))
                                p.inc(inst, "t")
                                for c in range(16):
                                    inst = te.matmul(
                                        gB[:, :],
                                        wb[:, 2048 + c * 128:
                                           2048 + (c + 1) * 128],
                                        h2T[:, c * 512:(c + 1) * 512],
                                        start=(c == 0), stop=(c == 15))
                                p.inc(inst, "t")
                                mm_done[ic] = p.cnt["t"]
                                ac.wait_ge(p.sem["t"], mm_done[ic])
                                if ic >= 1:
                                    ac.wait_ge(p.sem["v"], evac_done[ic - 1][1])
                                inst = ac.activation(sg_st[:, :], gA[:, :],
                                                     AF.Silu)
                                p.inc(inst, "a")
                                ve.wait_ge(p.sem["a"], p.cnt["a"])
                                inst = ve.tensor_tensor(
                                    hTm[:, ipx * 512:(ipx + 1) * 512],
                                    sg_st[:, :], gB[:, :], op=ALU.mult)
                                p.inc(inst, "v")
                                evac_done[ic] = (p.cnt["a"], p.cnt["v"])
                            p.wait(sy, "v")
                            wdnr = agw["wdn"][1].ap().rearrange(
                                "(m icg pp) c -> pp icg m c", m=8, icg=44)
                            for icx in range(11):
                                dma(sy, wdng.ap()[:, icx * 2048:
                                                  (icx + 1) * 2048]
                                    .rearrange("p (m c) -> p m c", m=8),
                                    wdnr[:, grp * 11 + icx, :, :], "s")
                            if grp == 3 and hs == 0 and tag == "a" \
                                    and "wdg" in tapd:
                                p.wait(gp, "s"); p.wait(gp, "v")
                                dma(gp, tapd["wdg"], wdng[:, :])
                                p.wait(gp, "g")
                            for k in range(4):
                                for m8 in range(8):
                                    pb = dn_ps[m8 % 2]
                                    p.wait(te, "s")
                                    p.wait(te, "v")
                                    for ipx in range(11):
                                        inst = te.matmul(
                                            pb[:, :],
                                            hTm[:, ipx * 512 + k * 128:
                                                ipx * 512 + (k + 1) * 128],
                                            wdng[:, (ipx * 8 + m8) * 256:
                                                 (ipx * 8 + m8 + 1) * 256],
                                            start=(ipx == 0), stop=(ipx == 10))
                                    p.inc(inst, "t")
                                    p.wait(ve, "t")
                                    ya = y_acc[:, k * 2048 + m8 * 256:
                                               k * 2048 + (m8 + 1) * 256]
                                    if grp == 0:
                                        V(ve.tensor_copy(ya, pb[:, :]))
                                    else:
                                        V(ve.tensor_tensor(ya, ya, pb[:, :],
                                                           op=ALU.add))
                        if hs == 0 and tag == "a" and "fhm" in tapd:
                            p.wait(gp, "v")
                            dma(gp, tapd["fhm"], hTm[:, :])
                            p.wait(gp, "g")
                        if hs == 0 and tag == "a" and "fya" in tapd:
                            p.wait(gp, "v"); p.wait(gp, "a"); p.wait(gp, "t")
                            dma(gp, tapd["fya"], y_acc[:, :])
                            p.wait(gp, "g")
                        for k, j in enumerate(jt):
                            emit_rstd(y_acc[:, k * 2048:(k + 1) * 2048],
                                      mean_sub=False, eps=1e-6)
                            V(ve.scalar_tensor_tensor(
                                x_sb[:, j * E:(j + 1) * E],
                                y_acc[:, k * 2048:(k + 1) * 2048],
                                col[:, 8:9], x_sb[:, j * E:(j + 1) * E],
                                op0=ALU.mult, op1=ALU.add))

            emit_qffn("a")
            fence()
            if "x2" in tapd:
                p.wait(gp, "v"); p.wait(gp, "a"); p.wait(gp, "t")
                for j in range(8):
                    dma(gp, tapd["x2"][j * 128:(j + 1) * 128, :],
                        x_sb[:, j * E:(j + 1) * E])
                p.wait(gp, "g")
            if STOP_AFTER == "ffn1":
                final_output_and_done()
                return nc

            fence()
            u0_guard = nc.sbuf_tensor("u0", [128, 8 * E], F32, side="right")
            u0 = u0_guard.__enter__()
            with ExitStack() as ph:
                en = ph.enter_context
                n0T = en(nc.sbuf_tensor("n0T", [128, 16 * TL], BF16))
                lnw_bc = en(nc.sbuf_tensor("lnw_bc1", [128, E], BF16))
                wib = [en(nc.sbuf_tensor(f"wib{i}", [128, 2048], BF16))
                       for i in range(2)]
                tr_psf = [en(nc.psum_tensor(_uname("ps"), [128, 1024], BF16))
                          for _ in range(2)]
                ip_ps = [en(nc.psum_tensor(_uname("ps"), [128, 512], F32)) for _i in range(4)]
                bc_ps = en(nc.psum_tensor(_uname("ps"), [128, 512], F32))
                shared["tr_ps"] = tr_psf

                emit_ln_bc(ln1_in, lnw_bc, bc_ps)
                for j in range(8):
                    p.wait(ve, "t")
                    emit_rstd(x_sb[:, j * E:(j + 1) * E], mean_sub=True,
                              eps=1e-5)
                    V(ve.tensor_scalar(u0[:, j * E:(j + 1) * E],
                                       x_sb[:, j * E:(j + 1) * E],
                                       col[:, 2:3], col[:, 8:9],
                                       op0=ALU.subtract, op1=ALU.mult))
                    emit_rstd(u0[:, j * E:(j + 1) * E], mean_sub=False,
                              eps=1e-5)
                    V(ve.tensor_scalar_mul(h_nat[:, :],
                                           u0[:, j * E:(j + 1) * E],
                                           col[:, 8:9]))
                    V(ve.tensor_tensor(h_nat[:, :], h_nat[:, :], lnw_bc[:, :],
                                       op=ALU.mult))
                    nat_to_T(h_nat[:, :],
                             lambda c, j=j: n0T[:, c * TL + j * 128:
                                                c * TL + (j + 1) * 128])
                for j in range(8):
                    dma(gp, x2_spill.ap()[j * 128:(j + 1) * 128, :],
                        x_sb[:, j * E:(j + 1) * E])

                sy.wait_ge(ag_sems["w_in"], 1)
                ipmm = {}
                ipev = {}
                for pc in range(81):
                    wb = wib[pc % 2]
                    if pc >= 2:
                        sy.wait_ge(p.sem["t"], ipmm[pc - 2])
                    else:
                        p.wait(sy, "t")
                    dma(sy, wb.ap().rearrange("p (c w) -> p c w", c=16),
                        agw["w_in"][1].ap().rearrange("(c p) n -> p c n", p=128)
                        [:, :, pc * 128:(pc + 1) * 128], "s")
                    lds = p.cnt["s"]
                    p.wait(ac, "g")
                    for tc in range(2):
                        pb = ip_ps[2 * (pc % 2) + tc]
                        te.wait_ge(p.sem["s"], lds)
                        if pc >= 2:
                            te.wait_ge(p.sem["a"], ipev[pc - 2])
                        else:
                            p.wait(te, "a")
                        for c in range(16):
                            inst = te.matmul(
                                pb[:, :], wb[:, c * 128:(c + 1) * 128],
                                n0T[:, c * TL + tc * 512:c * TL + (tc + 1) * 512],
                                start=(c == 0), stop=(c == 15))
                        p.inc(inst, "t")
                        if tc == 1:
                            ipmm[pc] = p.cnt["t"]
                        p.wait(ac, "t")
                        A(ac.copy(stg[:, tc * 512:(tc + 1) * 512], pb[:, :]))
                    ipev[pc] = p.cnt["a"]
                    p.wait(gp, "a")
                    if pc < 32:
                        dst = z_spill.ap()[pc * 128:(pc + 1) * 128, :]
                    elif pc < 40:
                        c = pc - 32
                        dst = a2a_sc_i.ap()[c * 768:c * 768 + 128, :]
                    elif pc < 48:
                        c = pc - 40
                        dst = a2a_sc_i.ap()[c * 768 + 128:c * 768 + 256, :]
                    elif pc < 80:
                        c = pc - 48
                        j2 = c // 4
                        dst = a2a_sc_i.ap()[j2 * 768 + 256 + (c % 4) * 128:
                                            j2 * 768 + 256 + (c % 4 + 1) * 128, :]
                    else:
                        dst = ag_dt_i.ap()[0:128, :]
                    dma(gp, dst, stg[:, 0:TL])
                p.wait(gp, "g")
                a2a_sc_sem = collective("AllToAll", [a2a_sc_i.ap().opt()],
                                        [a2a_sc_o.ap().opt()])
                ag_dt_sem = collective("AllGather", [ag_dt_i.ap().opt()],
                                       [ag_dt_o.ap().opt()])
            if "zx" in tapd:
                gp.wait_ge(a2a_sc_sem, 1)
                dma(gp, tapd["zx"], a2a_sc_o.ap())
                p.wait(gp, "g")
            if STOP_AFTER == "inproj":
                final_output_and_done()
                u0_guard.__exit__(None, None, None)
                return nc

        fence()
        with ExitStack() as ph:
            en = ph.enter_context
            dt_sl = en(nc.sbuf_tensor("dt_sl", [128, 8192], BF16))
            xc_sl = en(nc.sbuf_tensor("xc_sl", [128, 8192], BF16))
            u_c = en(nc.sbuf_tensor("u_c", [128, 2048], BF16))
            bt_bufs = [en(nc.sbuf_tensor(f"bt{i}", [128, 2048], BF16))
                       for i in range(2)]
            ct_bufs = [en(nc.sbuf_tensor(f"ct{i}", [128, 2048], BF16))
                       for i in range(2)]
            scr4 = en(nc.sbuf_tensor("scr4", [128, 8192], F32))
            y_acc4 = en(nc.sbuf_tensor("y_acc4", [128, 8192], F32))
            dtall = en(nc.sbuf_tensor("dtall", [128, 2048], BF16))
            xrep = en(nc.sbuf_tensor("xrep", [128, 2052], BF16))
            acol = en(nc.sbuf_tensor("acol", [128, 64], F32))
            dtb_sb = en(nc.sbuf_tensor("dtb_sb", [128, 4], F32))
            cvw_sb = en(nc.sbuf_tensor("cvw_sb", [128, 16], F32))
            cvb_sb = en(nc.sbuf_tensor("cvb_sb", [128, 4], F32))
            dpar_sb = en(nc.sbuf_tensor("dpar_sb", [128, 4], F32))
            carry = en(nc.sbuf_tensor("carry", [128, 64], F32))
            dt_ps = en(nc.psum_tensor(_uname("ps"), [128, 512], F32))

            arg_f = scr4[:, 0:2048]
            a_f = scr4[:, 2048:4096]
            b_f = scr4[:, 4096:6144]
            h_f = scr4[:, 6144:8192]
            spf = scr4[:, 0:1536]
            sprl = scr4[:, 1536:2048]
            y_acc = [y_acc4[:, c * 2048:(c + 1) * 2048] for c in range(4)]

            dma(gp, acol[:, :], alog_in)
            dma(gp, dtb_sb[:, :], dtb_in)
            dma(gp, cvw_sb[:, :], convw_in)
            dma(gp, cvb_sb[:, :], convb_in)
            dma(gp, dpar_sb[:, :], dpar_in)
            p.wait(ac, "g")
            A(ac.activation(acol[:, :], acol[:, :], AF.Exp))
            p.wait(ve, "a")
            V(ve.tensor_scalar_mul(acol[:, :], acol[:, :], -1.0))
            V(ve.memset(carry[:, :], 0.0))
            for c in range(4):
                V(ve.memset(y_acc[c], 0.0))

            gp.wait_ge(a2a_sc_sem, 1)
            gp.wait_ge(ag_dt_sem, 1)
            sy.wait_ge(a2a_sc_sem, 1)
            sy.wait_ge(ag_dt_sem, 1)
            hc_done = {}
            scan_done = {}
            bc_ready = {}
            a_bufs = [scr4[:, 2048:4096], scr4[:, 0:2048]]
            sc_o = a2a_sc_o.ap()

            for cc in range(4):
                p.wait(sy, "t")
                dma(sy, dtall.ap().rearrange("p (m t) -> p m t", m=2),
                    ag_dt_o.ap().rearrange("(m p) t -> p m t", p=128)
                    [:, 2 * cc:2 * cc + 2, :], "s")
                for dc in range(4):
                    for tc in range(4):
                        p.wait(te, "s")
                        p.wait(te, "v")
                        p.wait(te, "a")
                        inst = te.matmul(dt_ps[:, :],
                                         dtwT[:, dc * 128:(dc + 1) * 128],
                                         dtall[:, tc * 512:(tc + 1) * 512],
                                         start=True, stop=True)
                        p.inc(inst, "t")
                        p.wait(ac, "t")
                        A(ac.activation(spf[:, 0:512], dt_ps[:, :], AF.Abs,
                                        bias=dtb_sb[:, dc:dc + 1]))
                        A(ac.activation(spf[:, 512:1024], spf[:, 0:512],
                                        AF.Exp, scale=-1.0))
                        A(ac.activation(spf[:, 1024:1536], spf[:, 512:1024],
                                        AF.Ln, bias=1.0))
                        p.wait(ve, "a")
                        V(ve.tensor_scalar(sprl[:, :], dt_ps[:, :],
                                           dtb_sb[:, dc:dc + 1], 0.0,
                                           op0=ALU.add, op1=ALU.max))
                        V(ve.tensor_tensor(
                            dt_sl[:, dc * 2048 + tc * 512:
                                  dc * 2048 + (tc + 1) * 512],
                            spf[:, 1024:1536], sprl[:, :], op=ALU.add))
                for qd in range(4):
                    p.wait(gp, "v")
                    if cc == 0:
                        V(ve.memset(xrep[:, 0:3], 0.0))
                    else:
                        m0 = 2 * cc - 1
                        for av in range(2):
                            for bv in range(4):
                                dma(gp, xrep[av * 64 + bv * 16:
                                             av * 64 + bv * 16 + 16, 0:3],
                                    sc_o[768 * m0 + 32 * qd + 16 * av:
                                         768 * m0 + 32 * qd + 16 * av + 16,
                                         TL - 3:TL])
                    for mi in range(2):
                        m = 2 * cc + mi
                        for av in range(2):
                            for bv in range(4):
                                dma(gp, xrep[av * 64 + bv * 16:
                                             av * 64 + bv * 16 + 16,
                                             3 + mi * TL:3 + (mi + 1) * TL],
                                    sc_o[768 * m + 32 * qd + 16 * av:
                                         768 * m + 32 * qd + 16 * av + 16,
                                         0:TL])
                    p.wait(ve, "g")
                    p.wait(ve, "a")
                    V(ve.tensor_scalar_mul(arg_f[:, :], xrep[:, 0:2048],
                                           cvw_sb[:, qd * 4:qd * 4 + 1]))
                    for jj in range(1, 4):
                        V(ve.scalar_tensor_tensor(
                            arg_f[:, :], xrep[:, jj:jj + 2048],
                            cvw_sb[:, qd * 4 + jj:qd * 4 + jj + 1],
                            arg_f[:, :], op0=ALU.mult, op1=ALU.add))
                    p.wait(ac, "v")
                    A(ac.activation(xc_sl[:, qd * 2048:(qd + 1) * 2048],
                                    arg_f[:, :], AF.Silu,
                                    bias=cvb_sb[:, qd:qd + 1]))
                p.wait(ve, "a")

                def iter_cn(itx):
                    ccx, rem = divmod(itx, 64)
                    return (ccx,) + divmod(rem, 16)

                def prefetch_bc(itx):
                    ccx, cx, nx = iter_cn(itx)
                    bix = itx % 2
                    if itx - 2 in hc_done:
                        gp.wait_ge(p.sem["v"], hc_done[itx - 2])
                        sy.wait_ge(p.sem["v"], hc_done[itx - 2])
                    bsv = _bsrc(sc_o, ccx, cx, nx)
                    csv = _csrc(sc_o, ccx, cx, nx)
                    for av in range(2):
                        dma(gp, bt_bufs[bix][av * 64:(av + 1) * 64, :]
                            .rearrange("p (m t) -> p m t", m=2),
                            bsv[av, :, :, :])
                    for av in range(8):
                        dma(sy, ct_bufs[bix][av * 16:(av + 1) * 16, :]
                            .rearrange("p (m t) -> p m t", m=2),
                            csv[av, :, :, :], "s")
                    bc_ready[itx] = (p.cnt["g"], p.cnt["s"])

                if cc == 0:
                    prefetch_bc(0)
                for c in range(4):
                    V(ve.tensor_mul(u_c[:, :],
                                    dt_sl[:, c * 2048:(c + 1) * 2048],
                                    xc_sl[:, c * 2048:(c + 1) * 2048]))
                    for n in range(16):
                        it = cc * 64 + c * 16 + n
                        bi = it % 2
                        if it + 1 < 256:
                            prefetch_bc(it + 1)
                        if it > 0:
                            ac.wait_ge(p.sem["v"], scan_done[it - 1])
                        inst = ac.activation(a_bufs[it % 2][:, :],
                                             dt_sl[:, c * 2048:(c + 1) * 2048],
                                             AF.Exp,
                                             scale=acol[:, c * 16 + n:
                                                        c * 16 + n + 1])
                        p.inc(inst, "a")
                        ve.wait_ge(p.sem["g"], bc_ready[it][0])
                        ve.wait_ge(p.sem["s"], bc_ready[it][1])
                        p.wait(ve, "a")
                        V(ve.tensor_tensor(b_f[:, :], u_c[:, :],
                                           bt_bufs[bi][:, :], op=ALU.mult))
                        V(ve.tensor_tensor_scan(h_f[:, :], a_bufs[it % 2][:, :],
                                                b_f[:, :],
                                                carry[:, c * 16 + n:
                                                      c * 16 + n + 1],
                                                op0=ALU.mult, op1=ALU.add))
                        scan_done[it] = p.cnt["v"]
                        V(ve.tensor_copy(carry[:, c * 16 + n:c * 16 + n + 1],
                                         h_f[:, 2047:2048]))
                        V(ve.tensor_tensor(b_f[:, :], h_f[:, :],
                                           ct_bufs[bi][:, :], op=ALU.mult))
                        hc_done[it] = p.cnt["v"]
                        V(ve.tensor_tensor(y_acc[c], y_acc[c], b_f[:, :],
                                           op=ALU.add))
                for c in range(4):
                    V(ve.scalar_tensor_tensor(
                        y_acc[c], xc_sl[:, c * 2048:(c + 1) * 2048],
                        dpar_sb[:, c:c + 1], y_acc[c],
                        op0=ALU.mult, op1=ALU.add))
                    p.wait(gp, "v")
                    for mi in range(2):
                        j2 = 2 * cc + mi
                        dma(gp, a2a_y_i.ap()[j2 * DSL + c * 128:
                                             j2 * DSL + (c + 1) * 128, :],
                            y_acc[c][:, mi * 1024:(mi + 1) * 1024])
                    p.wait(ve, "g")
                    V(ve.memset(y_acc[c], 0.0))
            p.wait(gp, "g")
            a2a_y_sem = collective("AllToAll", [a2a_y_i.ap().opt()],
                                   [a2a_y_o.ap().opt()])
        if "ys" in tapd:
            gp.wait_ge(a2a_y_sem, 1)
            dma(gp, tapd["ys"], a2a_y_o.ap())
            p.wait(gp, "g")
        if STOP_AFTER == "scan":
            fence()
            with ExitStack() as ph:
                x_sb = ph.enter_context(nc.sbuf_tensor("x_sb2",
                                                       [128, 8 * E], F32))
                shared["x"] = x_sb
                for j in range(8):
                    dma(sy, x_sb[:, j * E:(j + 1) * E],
                        x2_spill.ap()[j * 128:(j + 1) * 128, :], "s")
                p.wait(sy, "s")
                final_output_and_done()
            u0_guard.__exit__(None, None, None)
            return nc

        fence()
        with ExitStack() as phx:
            with ExitStack() as ph:
                en = ph.enter_context
                yT = en(nc.sbuf_tensor("yT", [128, 32 * 1024], BF16))
                wop_b = [en(nc.sbuf_tensor(f"wop{i}", [128, 2048], BF16))
                         for i in range(2)]
                zst = en(nc.sbuf_tensor("zst", [128, 1024], BF16))
                zsl = en(nc.sbuf_tensor("zsl", [128, 1024], BF16))
                op_ps = [en(nc.psum_tensor(_uname("ps"), [128, 512], F32)) for _i in range(4)]

                gp.wait_ge(a2a_y_sem, 1)
                ysrc = a2a_y_o.ap().rearrange("(m f p) t -> p m f t", m=8, f=4)
                for m in range(8):
                    dma(gp, yT.ap().rearrange("p (m f t) -> p m f t",
                                              m=8, f=4)[:, m, :, :],
                        ysrc[:, m, :, :])
                p.wait(gp, "g")
                for d32 in range(32):
                    p.wait(gp, "a")
                    dma(gp, zst[:, :],
                        z_spill.ap()[d32 * 128:(d32 + 1) * 128, :])
                    p.wait(ac, "g")
                    A(ac.activation(zsl[:, :], zst[:, :], AF.Silu))
                    p.wait(ve, "a")
                    V(ve.tensor_tensor(yT[:, d32 * 1024:(d32 + 1) * 1024],
                                       yT[:, d32 * 1024:(d32 + 1) * 1024],
                                       zsl[:, :], op=ALU.mult))
                sy.wait_ge(ag_sems["outp"], 1)
                p.wait(sy, "v")
                tcount = {}
                for tt in range(8):
                    for d32 in range(32):
                        wb = wop_b[d32 % 2]
                        if (tt, d32 - 2) in tcount:
                            sy.wait_ge(p.sem["t"], tcount[(tt, d32 - 2)])
                        elif tt > 0 and d32 < 2:
                            sy.wait_ge(p.sem["t"], p.cnt["t"])
                        dma(sy, wb[:, :],
                            agw["outp"][1].ap()[d32 * 128:(d32 + 1) * 128, :],
                            "s")
                        p.wait(te, "s")
                        if d32 == 0:
                            p.wait(te, "v")
                        for ec in range(4):
                            inst = te.matmul(
                                op_ps[ec][:, :],
                                yT[:, d32 * 1024 + tt * 128:
                                   d32 * 1024 + (tt + 1) * 128],
                                wb[:, ec * 512:(ec + 1) * 512],
                                start=(d32 == 0), stop=(d32 == 31))
                            if ec == 3:
                                p.inc(inst, "t")
                        tcount[(tt, d32)] = p.cnt["t"]
                    p.wait(ve, "t")
                    for ec in range(4):
                        V(ve.tensor_tensor(u0[:, tt * E + ec * 512:
                                              tt * E + (ec + 1) * 512],
                                           u0[:, tt * E + ec * 512:
                                              tt * E + (ec + 1) * 512],
                                           op_ps[ec][:, :], op=ALU.add))
                if "r1" in tapd:
                    p.wait(gp, "v"); p.wait(gp, "a"); p.wait(gp, "t")
                    for j in range(8):
                        dma(gp, tapd["r1"][j * 128:(j + 1) * 128, :],
                            u0[:, j * E:(j + 1) * E])
                    p.wait(gp, "g")

            fence()
            x_sb = phx.enter_context(nc.sbuf_tensor("x_sb2", [128, 8 * E], F32))
            shared["x"] = x_sb
            for j in range(8):
                dma(sy, x_sb[:, j * E:(j + 1) * E],
                    x2_spill.ap()[j * 128:(j + 1) * 128, :], "s")
            p.wait(ve, "s")
            for j in range(8):
                V(ve.tensor_tensor(x_sb[:, j * E:(j + 1) * E],
                                   x_sb[:, j * E:(j + 1) * E],
                                   u0[:, j * E:(j + 1) * E], op=ALU.add))

            with ExitStack() as ph:
                en = ph.enter_context
                h2T = en(nc.sbuf_tensor("mh2T", [128, 16 * 1024], BF16))
                lnw_bc = en(nc.sbuf_tensor("lnw_bc2", [128, E], BF16))
                tr_psf = [en(nc.psum_tensor(_uname("ps"), [128, 1024], BF16))
                          for _ in range(2)]
                bc_ps = en(nc.psum_tensor(_uname("ps"), [128, 512], F32))
                shared["tr_ps"] = tr_psf
                emit_ln_bc(ln2_in, lnw_bc, bc_ps)

                for j in range(8):
                    hs, k = j // 4, j % 4
                    p.wait(ve, "t")
                    emit_rstd(u0[:, j * E:(j + 1) * E], mean_sub=False,
                              eps=1e-5)
                    V(ve.tensor_scalar_mul(h_nat[:, :],
                                           u0[:, j * E:(j + 1) * E],
                                           col[:, 8:9]))
                    V(ve.tensor_tensor(h_nat[:, :], h_nat[:, :],
                                       lnw_bc[:, :], op=ALU.mult))
                    nat_to_T(h_nat[:, :],
                             lambda c, hs=hs, k=k:
                             h2T[:, c * 1024 + hs * 512 + k * 128:
                                 c * 1024 + hs * 512 + (k + 1) * 128])
                p.wait(gp, "v")
                u0_guard.__exit__(None, None, None)

                fence()
                for hs in range(2):
                    jt = [4 * hs + k for k in range(4)]
                    with ExitStack() as ph2:
                        en2 = ph2.enter_context
                        hTm = en2(nc.sbuf_tensor(f"mhTm{hs}", [128, 8 * 512],
                                                 BF16))
                        dng = en2(nc.sbuf_tensor(f"dng{hs}", [128, 8 * 2048],
                                                 BF16))
                        wgu_b = [en2(nc.sbuf_tensor(f"mwgu{hs}{i}", [128, 4096],
                                                    BF16)) for i in range(2)]
                        sg_st = en2(nc.sbuf_tensor(f"msg_st{hs}", [128, 512],
                                                   BF16))
                        gu_ps = [en2(nc.psum_tensor(_uname("ps"), [128, 512], F32))
                                 for _ in range(2)]
                        dn_ps = [en2(nc.psum_tensor(_uname("ps"), [128, 512], F32))
                                 for _ in range(2)]
                        gp.wait_ge(ag_sems["gate"], 1)
                        gp.wait_ge(ag_sems["up"], 1)
                        gp.wait_ge(ag_sems["down"], 1)
                        sy.wait_ge(ag_sems["gate"], 1)
                        sy.wait_ge(ag_sems["up"], 1)
                        sy.wait_ge(ag_sems["down"], 1)
                        mm_done = {}
                        evac_done = {}
                        for grp in range(8):
                            for ipx in range(8):
                                ic = grp * 8 + ipx
                                wb = wgu_b[ic % 2]
                                if ic >= 2:
                                    sy.wait_ge(p.sem["t"], mm_done[ic - 2])
                                else:
                                    p.wait(sy, "t")
                                dma(sy, wb[:, 0:2048].rearrange(
                                        "p (c w) -> p c w", c=16),
                                    agw["gate"][1].ap().rearrange(
                                        "(c p) n -> p c n", p=128)
                                    [:, :, ic * 128:(ic + 1) * 128], "s")
                                dma(sy, wb[:, 2048:4096].rearrange(
                                        "p (c w) -> p c w", c=16),
                                    agw["up"][1].ap().rearrange(
                                        "(c p) n -> p c n", p=128)
                                    [:, :, ic * 128:(ic + 1) * 128], "s")
                                te.wait_ge(p.sem["s"], p.cnt["s"])
                                if ic >= 1:
                                    te.wait_ge(p.sem["a"], evac_done[ic - 1][0])
                                    te.wait_ge(p.sem["v"], evac_done[ic - 1][1])
                                else:
                                    p.wait(te, "a")
                                    p.wait(te, "v")
                                for c in range(16):
                                    inst = te.matmul(
                                        gu_ps[0][:, :],
                                        wb[:, c * 128:(c + 1) * 128],
                                        h2T[:, c * 1024 + hs * 512:
                                            c * 1024 + (hs + 1) * 512],
                                        start=(c == 0), stop=(c == 15))
                                p.inc(inst, "t")
                                for c in range(16):
                                    inst = te.matmul(
                                        gu_ps[1][:, :],
                                        wb[:, 2048 + c * 128:
                                           2048 + (c + 1) * 128],
                                        h2T[:, c * 1024 + hs * 512:
                                            c * 1024 + (hs + 1) * 512],
                                        start=(c == 0), stop=(c == 15))
                                p.inc(inst, "t")
                                mm_done[ic] = p.cnt["t"]
                                ac.wait_ge(p.sem["t"], mm_done[ic])
                                if ic >= 1:
                                    ac.wait_ge(p.sem["v"], evac_done[ic - 1][1])
                                inst = ac.activation(sg_st[:, :], gu_ps[0][:, :],
                                                     AF.Silu)
                                p.inc(inst, "a")
                                ve.wait_ge(p.sem["a"], p.cnt["a"])
                                inst = ve.tensor_tensor(
                                    hTm[:, ipx * 512:(ipx + 1) * 512],
                                    sg_st[:, :], gu_ps[1][:, :], op=ALU.mult)
                                p.inc(inst, "v")
                                evac_done[ic] = (p.cnt["a"], p.cnt["v"])
                            p.wait(sy, "v")
                            dma(sy, dng.ap().rearrange("p (ic n) -> p ic n",
                                                       ic=8),
                                agw["down"][1].ap().rearrange(
                                    "(ic p) n -> p ic n", p=128)
                                [:, grp * 8:(grp + 1) * 8, :], "s")
                            for k in range(4):
                                for ec in range(4):
                                    pb = dn_ps[ec % 2]
                                    p.wait(te, "s")
                                    p.wait(te, "v")
                                    for ipx in range(8):
                                        inst = te.matmul(
                                            pb[:, :],
                                            hTm[:, ipx * 512 + k * 128:
                                                ipx * 512 + (k + 1) * 128],
                                            dng[:, ipx * 2048 + ec * 512:
                                                ipx * 2048 + (ec + 1) * 512],
                                            start=(ipx == 0), stop=(ipx == 7))
                                    p.inc(inst, "t")
                                    p.wait(ve, "t")
                                    j = jt[k]
                                    V(ve.tensor_tensor(
                                        x_sb[:, j * E + ec * 512:
                                             j * E + (ec + 1) * 512],
                                        x_sb[:, j * E + ec * 512:
                                             j * E + (ec + 1) * 512],
                                        pb[:, :], op=ALU.add))
            if "x3" in tapd:
                p.wait(gp, "v"); p.wait(gp, "a"); p.wait(gp, "t")
                for j in range(8):
                    dma(gp, tapd["x3"][j * 128:(j + 1) * 128, :],
                        x_sb[:, j * E:(j + 1) * E])
                p.wait(gp, "g")
            if STOP_AFTER == "gate":
                final_output_and_done()
                return nc

            fence()
            emit_qffn("b")
            final_output_and_done()
    return nc


def _xsrc(sc_o, m, qd):
    base = sc_o[768 * m + 32 * qd:768 * m + 32 * qd + 32, :]
    v = base.rearrange("(a c) t -> a c t", a=2)
    return v.unsqueeze(1).broadcast_to([2, 4, 16, v.shape[-1]])


def _bsrc(sc_o, cc, c, n):
    rows = sc_o.rearrange("(m r) t -> m r t", m=NC)
    v = rows[2 * cc:2 * cc + 2, 128 + 32 * c + n:128 + 32 * c + n + 17:16, :]
    v = v.transpose([1, 0, 2])
    return v.unsqueeze(1).broadcast_to([2, 64, 2, v.shape[-1]])


def _csrc(sc_o, cc, c, n):
    rows = sc_o.rearrange("(m r) t -> m r t", m=NC)
    v = rows[2 * cc:2 * cc + 2, 256 + 128 * c + n:256 + 128 * c + n + 113:16, :]
    v = v.transpose([1, 0, 2])
    return v.unsqueeze(1).broadcast_to([8, 16, 2, v.shape[-1]])



def kernel(**inputs):
    f32 = np.float32
    x = np.ascontiguousarray(np.asarray(inputs["L0_attn_a_x"], f32)[0])
    Wq = np.asarray(inputs["L0_attn_a_Wq"], f32)
    Wk = np.asarray(inputs["L0_attn_a_Wk"], f32)
    Wv = np.asarray(inputs["L0_attn_a_Wv"], f32)
    Wo = np.asarray(inputs["L0_attn_a_Wo"], f32)
    w_in = np.asarray(inputs["L0_attn_b_in_proj_w"], f32)
    gate_w = np.asarray(inputs["L0_attn_b_gate_w"], f32)
    up_w = np.asarray(inputs["L0_attn_b_up_w"], f32)
    down_w = np.asarray(inputs["L0_attn_b_down_w"], f32)
    outp_w = np.asarray(inputs["L0_attn_b_out_proj_w"], f32)
    gup = np.asarray(inputs["L0_ffn_gate_up_packed"], np.int32)
    gus = np.asarray(inputs["L0_ffn_gate_up_scales"])
    dnp = np.asarray(inputs["L0_ffn_down_packed"], np.int32)
    dns = np.asarray(inputs["L0_ffn_down_scales"])
    dtw = np.asarray(inputs["L0_attn_b_dt_proj_w"], f32)
    dtb = np.asarray(inputs["L0_attn_b_dt_proj_b"], f32)
    cw = np.asarray(inputs["L0_attn_b_conv_w"], f32)[:, 0, :]
    cb = np.asarray(inputs["L0_attn_b_conv_b"], f32)
    alog = np.asarray(inputs["L0_attn_b_A_log"], f32)
    dpar = np.asarray(inputs["L0_attn_b_D_param"], f32)
    ln1 = np.asarray(inputs["L0_attn_b_ln1_w"], f32)
    ln2 = np.asarray(inputs["L0_attn_b_ln2_w"], f32)
    lq1 = np.asarray(inputs["L0_attn_a_lq1"], f32)
    lk1 = np.asarray(inputs["L0_attn_a_lk1"], f32)
    lq2 = np.asarray(inputs["L0_attn_a_lq2"], f32)
    lk2 = np.asarray(inputs["L0_attn_a_lk2"], f32)

    gusT = np.ascontiguousarray(gus.T)
    dnsT = np.ascontiguousarray(dns.T)
    iden = np.eye(128, dtype=f32)

    in_maps = []
    for i in range(NC):
        sl256 = slice(256 * i, 256 * (i + 1))
        sl512 = slice(512 * i, 512 * (i + 1))
        m = {
            "x0": np.ascontiguousarray(x[TL * i:TL * (i + 1)]),
            "wq_s": np.ascontiguousarray(Wq[sl256]),
            "wk_s": np.ascontiguousarray(Wk[sl256]),
            "wv_s": np.ascontiguousarray(Wv[sl256]),
            "wo_s": np.ascontiguousarray(Wo[sl256]),
            "w_in_s": np.ascontiguousarray(w_in[:, sl256]),
            "gate_s": np.ascontiguousarray(gate_w[:, sl256]),
            "up_s": np.ascontiguousarray(up_w[:, sl256]),
            "down_s": np.ascontiguousarray(down_w[:, 1024 * i:1024 * (i + 1)]),
            "outp_s": np.ascontiguousarray(outp_w[:, sl512]),
            "gup_s": np.ascontiguousarray(gup[:, 32 * i:32 * (i + 1)]),
            "gus_s": np.ascontiguousarray(gusT[:, 2 * i:2 * (i + 1)]),
            "dnp_s": np.ascontiguousarray(dnp[sl256]),
            "dns_s": np.ascontiguousarray(dnsT[sl256]),
            "dtw_s": np.ascontiguousarray(dtw[sl512]),
            "convw_s": np.ascontiguousarray(
                cw[sl512].reshape(4, 128, 4).transpose(1, 0, 2).reshape(128, 16)),
            "convb_s": np.ascontiguousarray(
                cb[sl512].reshape(4, 128).T.reshape(128, 4)),
            "alog_s": np.ascontiguousarray(
                alog[sl512].reshape(4, 128, 16).transpose(1, 0, 2).reshape(128, 64)),
            "dpar_s": np.ascontiguousarray(
                dpar[sl512].reshape(4, 128).T.reshape(128, 4)),
            "dtb_s": np.ascontiguousarray(
                dtb[sl512].reshape(4, 128).T.reshape(128, 4)),
            "ln1": ln1.reshape(1, E), "ln2": ln2.reshape(1, E),
            "lq1": lq1.reshape(1, HD), "lk1": lk1.reshape(1, HD),
            "lq2": lq2.reshape(1, HD), "lk2": lk2.reshape(1, HD),
            "iden": iden,
        }
        in_maps.append(m)

    nc_prog = build_program()
    res = run_bass_kernel_spmd(nc_prog, in_maps, core_ids=list(range(NC)))
    out = np.concatenate([np.asarray(r["out"], np.float32)
                          for r in res.results], axis=0)
    return out.reshape(1, T, E)
